# version 11
# baseline (speedup 1.0000x reference)
import numpy as np
import concourse.bass as bass
import concourse.mybir as mybir
from concourse.bass_utils import run_bass_kernel_spmd

F32 = mybir.dt.float32
BF16 = mybir.dt.bfloat16
I32 = mybir.dt.int32
AF = mybir.ActivationFunctionType
ALU = mybir.AluOpType

D = 1024
T = 2048
DI = 2048
DST = 16
DTR = 64
HQ = 3072
HF = 4096
EPS = 1e-5
QSC = 1.0 / 127.0
NCORES = 8

PC_D = D // 128
PC_DI = DI // 128
PC_HQ = HQ // 128
PC_HF = HF // 128

DT_SIZE = {F32: 4, BF16: 2, I32: 4}

ENGINES = ("sync", "scalar", "vector", "tensor", "gpsimd")


class Op:
    __slots__ = ("eng", "fn", "deps", "dma", "needed", "val", "pos")

    def __init__(self, eng, fn, deps, dma):
        self.eng = eng
        self.fn = fn
        self.deps = deps
        self.dma = dma
        self.needed = dma
        self.val = None
        self.pos = None


def _flat(deps):
    out = []
    stack = [deps]
    while stack:
        d = stack.pop()
        if d is None:
            continue
        if isinstance(d, Op):
            out.append(d)
        elif isinstance(d, (list, tuple)):
            stack.extend(d)
        else:
            raise TypeError(f"bad dep {d!r}")
    return out


class Prog:
    def __init__(self):
        self.ops = {e: [] for e in ENGINES}
        self.all_ops = []

    def op(self, eng, fn, deps=()):
        o = Op(eng, fn, _flat(deps), False)
        o.pos = len(self.ops[eng])
        self.ops[eng].append(o)
        self.all_ops.append(o)
        return o

    def dma(self, eng, fn, deps=()):
        o = Op(eng, fn, _flat(deps), True)
        o.pos = len(self.ops[eng])
        self.ops[eng].append(o)
        self.all_ops.append(o)
        return o

    def fence(self, eng, deps):
        return self.op(eng, None, deps)

    def finalize_and_emit(self, nc):
        for o in self.all_ops:
            for d in o.deps:
                assert d.fn is not None, "op depends on a fence (never incs)"
                if d.eng != o.eng or d.dma or o.dma:
                    d.needed = True
        cnt = {}
        for e in ENGINES:
            c = 0
            d = 0
            for o in self.ops[e]:
                if o.dma:
                    d += 1
                    o.val = 16 * d
                elif o.needed and o.fn is not None:
                    c += 1
                    o.val = c
            cnt[e] = (c, d)
        from contextlib import ExitStack
        with ExitStack() as st:
            sems = {}
            for e in ENGINES:
                c, d = cnt[e]
                if c > 0:
                    sems["c" + e] = st.enter_context(nc.semaphore("sem_c_" + e))
                if d > 0:
                    sems["d" + e] = st.enter_context(nc.semaphore("sem_d_" + e))
            block = st.enter_context(nc.Block())
            for e in ENGINES:
                if self.ops[e]:
                    self._emit_engine(block, e, sems)

    def _emit_engine(self, block, e, sems):
        ops = self.ops[e]

        def body(eng):
            hw = {}
            for o in ops:
                for dep in o.deps:
                    if dep.val is None:
                        continue
                    if dep.eng == e and not dep.dma and not o.dma \
                            and e != "vector":
                        continue
                    key = ("d" if dep.dma else "c") + dep.eng
                    if hw.get(key, 0) >= dep.val:
                        continue
                    eng.wait_ge(sems[key], dep.val)
                    hw[key] = dep.val
                if o.fn is None:
                    continue
                inst = o.fn(eng)
                if o.dma:
                    inst.then_inc(sems["d" + e], 16)
                elif o.needed:
                    inst.then_inc(sems["c" + e], 1)

        getattr(block, e)(body)


class SB:

    def __init__(self, nc):
        lo = (nc.sbuf_base + 63) & ~63
        hi = nc.sbuf_top
        self.nc = nc
        self.freelist = [(lo, hi, [])]
        self.ranges = {}
        self.n = 0

    def alloc(self, name, shape, dtype):
        bpp = int(np.prod(shape[1:])) * DT_SIZE[dtype]
        bpp = (bpp + 63) & ~63
        for i, (a, b, stale) in enumerate(self.freelist):
            if b - a >= bpp:
                self.freelist[i] = (a + bpp, b, stale)
                if self.freelist[i][0] == self.freelist[i][1]:
                    del self.freelist[i]
                self.n += 1
                h = self.nc.alloc_sbuf_tensor_at(
                    f"{name}_{self.n}", list(shape), dtype, offset=a)
                self.ranges[id(h)] = (a, a + bpp)
                return h, list(stale)
        raise MemoryError(f"SBUF alloc failed for {name} {shape} ({bpp}B/p); "
                          f"free={[(a, b) for a, b, _ in self.freelist]}")

    def free(self, h, tokens):
        a, b = self.ranges.pop(id(h))
        self.freelist.append((a, b, list(tokens)))
        self.freelist.sort(key=lambda s: s[0])
        merged = []
        for seg in self.freelist:
            if merged and merged[-1][1] == seg[0]:
                merged[-1] = (merged[-1][0], seg[1], merged[-1][2] + seg[2])
            else:
                merged.append((seg[0], seg[1], list(seg[2])))
        self.freelist = merged


class Buf:
    __slots__ = ("h", "wr", "rd", "sb")

    def __init__(self, h, stale, sb):
        self.h = h
        self.wr = list(stale)
        self.rd = []
        self.sb = sb

    def begin_write(self):
        d = self.rd + self.wr
        self.wr = []
        self.rd = []
        return d

    def wrote(self, tok):
        self.wr.append(tok)

    def read(self, tok):
        self.rd.append(tok)

    def free(self):
        self.sb.free(self.h, self.rd + self.wr)


def _psbuf(h):
    b = Buf.__new__(Buf)
    b.h = h
    b.wr = []
    b.rd = []
    b.sb = None
    return b


class PsumPool:
    def __init__(self, nc):
        self.banks = [_psbuf(nc.alloc_psum_tensor(f"pb{i}", [128, 512], F32))
                      for i in range(7)]
        self.bT = _psbuf(nc.alloc_psum_tensor("pbT", [128, 512], BF16))
        self.i = 0

    def acquire(self):
        b = self.banks[self.i % len(self.banks)]
        self.i += 1
        return b


def build(phases=("p2", "p3", "p4", "p5", "p6", "p7"), debug=False):
    phases = set(phases)
    nc = bass.Bass("TRN2", target_bir_lowering=False)
    p = Prog()
    sb = SB(nc)
    pp = PsumPool(nc)

    def din(name, shape, dtype=F32):
        return nc.dram_tensor(name, list(shape), dtype,
                              kind="ExternalInput").ap()

    def dspill(name, shape, dtype):
        kind = "ExternalOutput" if debug else "Internal"
        return nc.dram_tensor(name, list(shape), dtype, kind=kind).ap()

    xin = din("xin", [128, 2, 8, 1024])
    inw = {1: din("inw1", [128, 16, 8, 256]), 2: din("inw2", [128, 16, 8, 256])}
    outw = {1: din("outw1", [128, 8, 16, 128]), 2: din("outw2", [128, 8, 16, 128])}
    xpd = {1: din("xpd1", [128, 16, DTR]), 2: din("xpd2", [128, 16, DTR])}
    xpb = {1: din("xpb1", [128, 16, DST]), 2: din("xpb2", [128, 16, DST])}
    xpc = {1: din("xpc1", [128, 16, DST]), 2: din("xpc2", [128, 16, DST])}
    dtw = {1: din("dtw1", [DTR, 16, 128]), 2: din("dtw2", [DTR, 16, 128])}
    convw = {1: din("conv1", [128, 16, 4]), 2: din("conv2", [128, 16, 4])}
    cbv = {1: din("cb1", [128, 16]), 2: din("cb2", [128, 16])}
    dtbv = {1: din("dtb1", [128, 16]), 2: din("dtb2", [128, 16])}
    alogv = {1: din("alog1", [128, 16, DST]), 2: din("alog2", [128, 16, DST])}
    dpv = {1: din("dp1", [128, 16]), 2: din("dp2", [128, 16])}
    ng = din("ng", [128, 8]); nb = din("nb", [128, 8])
    fg = din("fg", [128, 8]); fb = din("fb", [128, 8])
    ff1 = din("ff1", [128, 8, 8, 512]); f1bi = din("f1b", [128, 32])
    ff2 = din("ff2", [128, 2, 32, 512]); f2bi = din("f2b", [128, 8])
    wg = din("wg", [128, 12, 8, 256], I32)
    wu = din("wu", [128, 12, 8, 256], I32)
    wd = din("wd", [128, 24, 1024], I32)
    eyeF = din("eyeF", [128, 128])
    jmatF = din("jmatF", [128, 128])
    selF = din("selF", [DST, DST, 128])

    out = nc.dram_tensor("out", [128, 2, 8, 1024], F32,
                         kind="ExternalOutput").ap()
    xspA = dspill("xspA", [128, 2, 8, 1024], F32)
    h2sp = dspill("h2sp", [128, 4, 8, 512], BF16)
    xasp = dspill("xasp", [128, 4, 8, 512], F32)
    fwdsp = dspill("fwdsp", [128, 4, 8, 512], BF16)
    xspB = dspill("xspB", [128, 4, 8, 512], F32)

    dram_tok = {"xspA": [[] for _ in range(4)], "h2sp": [[] for _ in range(4)],
                "xasp": [[] for _ in range(4)], "fwdsp": [[] for _ in range(4)],
                "xspB": [[] for _ in range(4)]}

    def mk(name, shape, dtype):
        h, stale = sb.alloc(name, shape, dtype)
        return Buf(h, stale, sb)

    def dma_load(buf, src, deps=(), cast=False, dst=None):
        eng = "gpsimd" if cast else "sync"
        dstap = buf.h[:] if dst is None else dst
        d = buf.begin_write() + _flat(deps)
        tok = p.dma(eng, lambda e, dstap=dstap, src=src: e.dma_start(
            out=dstap, in_=src), deps=d)
        buf.wrote(tok)
        return tok

    def dma_load_into(buf, dst, src, deps=(), cast=False):
        eng = "gpsimd" if cast else "sync"
        d = list(buf.wr) + list(buf.rd) + _flat(deps)
        tok = p.dma(eng, lambda e, dst=dst, src=src: e.dma_start(
            out=dst, in_=src), deps=d)
        buf.wrote(tok)
        return tok

    def dma_store(dst, buf, deps=(), src=None, eng="sync"):
        srcap = buf.h[:] if src is None else src
        tok = p.dma(eng, lambda e, dst=dst, srcap=srcap: e.dma_start(
            out=dst, in_=srcap), deps=list(buf.wr) + _flat(deps))
        buf.read(tok)
        return tok

    def resident(name, shape, dtype, src, cast=False):
        b = mk(name, shape, dtype)
        dma_load(b, src, cast=cast)
        return b

    conv_sb = {k: resident(f"conv{k}", [128, 16, 4], F32, convw[k])
               for k in (1, 2)}
    cb_sb = {k: resident(f"cb{k}", [128, 16], F32, cbv[k]) for k in (1, 2)}
    dtb_sb = {k: resident(f"dtb{k}", [128, 16], F32, dtbv[k]) for k in (1, 2)}
    dp_sb = {k: resident(f"dp{k}", [128, 16], F32, dpv[k]) for k in (1, 2)}
    ng_sb = resident("ng", [128, 8], F32, ng)
    nb_sb = resident("nb", [128, 8], F32, nb)
    fg_sb = resident("fg", [128, 8], F32, fg)
    fb_sb = resident("fb", [128, 8], F32, fb)
    f1b_sb = resident("f1b", [128, 32], F32, f1bi)
    f2b_sb = resident("f2b", [128, 8], F32, f2bi)
    alog_sb = {k: resident(f"alog{k}", [128, 16, DST], F32, alogv[k])
               for k in (1, 2)}
    eye_sb = resident("eye", [128, 128], BF16, eyeF, cast=True)
    jmat_sb = resident("jmat", [128, 128], BF16, jmatF, cast=True)
    sel_sb = resident("sel", [DST, DST, 128], BF16, selF, cast=True)
    xpd_sb = {k: resident(f"xpd{k}", [128, 16, DTR], BF16, xpd[k], cast=True)
              for k in (1, 2)}
    xpb_sb = {k: resident(f"xpb{k}", [128, 16, DST], BF16, xpb[k], cast=True)
              for k in (1, 2)}
    xpc_sb = {k: resident(f"xpc{k}", [128, 16, DST], BF16, xpc[k], cast=True)
              for k in (1, 2)}
    dtw_sb = {k: resident(f"dtw{k}", [DTR, 16, 128], BF16, dtw[k], cast=True)
              for k in (1, 2)}

    ones_col_b = mk("ones_col_b", [128, 1], BF16)
    ones_col_b.wrote(p.op("vector", lambda e: e.memset(ones_col_b.h[:], 1.0)))
    ones_row_b = mk("ones_row_b", [1, 128], BF16)
    ones_row_b.wrote(p.op("vector", lambda e: e.memset(ones_row_b.h[:], 1.0)))
    eps_c = mk("eps_c", [1, 1], F32)
    eps_c.wrote(p.op("vector", lambda e: e.memset(eps_c.h[:], EPS)))
    one_c = mk("one_c", [128, 1], F32)
    one_c.wrote(p.op("vector", lambda e: e.memset(one_c.h[:], 1.0)))
    qsc_c = mk("qsc_c", [128, 1], F32)
    qsc_c.wrote(p.op("vector", lambda e: e.memset(qsc_c.h[:], QSC)))

    A_sb = {}
    for k in (1, 2):
        a = mk(f"A{k}", [128, 16, DST], F32)
        src = alog_sb[k]
        d = a.begin_write()
        t1 = p.op("scalar", lambda e, a=a, src=src: e.activation(
            out=a.h[:], in_=src.h[:], func=AF.Exp), deps=d + src.wr)
        src.read(t1)
        t2 = p.op("vector", lambda e, a=a: e.tensor_scalar_mul(
            a.h[:], a.h[:], -1.0), deps=[t1])
        a.wrote(t2)
        src.free()
        A_sb[k] = a

    lns = {}

    def lns_alloc():
        lns["scr_rows"] = mk("scr_rows", [1, 3, 512], F32)
        lns["scr_rows_b"] = mk("scr_rows_b", [1, 2, 512], BF16)
        lns["sq_t"] = mk("sq_t", [128, 2, 512], BF16)
        lns["xb_t"] = mk("xb_t", [128, 2, 512], BF16)
        lns["lnt"] = mk("lnt", [128, 2, 512], BF16)

    def lns_free():
        for k in list(lns):
            lns.pop(k).free()

    lns_alloc()

    def ln_512(xc, xc_view, src_bf16=False):
        scr_rows = lns["scr_rows"]
        scr_rows_b = lns["scr_rows_b"]
        sq_t = lns["sq_t"]
        xb_t = lns["xb_t"]
        psS = pp.acquire()
        psQ = pp.acquire()
        dS = psS.begin_write()
        dQ = psQ.begin_write()
        mm_s = mm_q = None
        for i in range(PC_D):
            par = i % 2
            sqv = sq_t.h[:, par, :]
            xbv = xb_t.h[:, par, :]
            if src_bf16:
                xv_b = xc_view(i)
                tcast = None
            else:
                dxb = xb_t.begin_write() if par == 0 and i < 2 else list(xb_t.wr) + list(xb_t.rd)
                tcast = p.op("scalar", lambda e, i=i, xbv=xbv: e.activation(
                    out=xbv, in_=xc_view(i), func=AF.Copy),
                    deps=dxb + xc.wr)
                xc.read(tcast)
                xb_t.wrote(tcast)
                xv_b = xbv
            dsq = sq_t.begin_write() if i < 2 else list(sq_t.wr) + list(sq_t.rd)
            tsq = p.op("scalar", lambda e, i=i, sqv=sqv: e.activation(
                out=sqv, in_=xc_view(i), func=AF.Square),
                deps=dsq + xc.wr)
            xc.read(tsq)
            sq_t.wrote(tsq)
            mm_s = p.op("tensor", lambda e, i=i, xv_b=xv_b: e.matmul(
                psS.h[0:1, :], ones_col_b.h[:], xv_b,
                start=(i == 0), stop=(i == PC_D - 1)),
                deps=dS + ([tcast] if tcast else xc.wr) + ones_col_b.wr)
            if tcast is None:
                xc.read(mm_s)
            else:
                xb_t.read(mm_s)
            mm_q = p.op("tensor", lambda e, i=i, sqv=sqv: e.matmul(
                psQ.h[0:1, :], ones_col_b.h[:], sqv,
                start=(i == 0), stop=(i == PC_D - 1)),
                deps=dQ + [tsq] + ones_col_b.wr)
            sq_t.read(mm_q)
        psS.wrote(mm_s)
        psQ.wrote(mm_q)
        mu = scr_rows.h[:, 0, :]
        q = scr_rows.h[:, 1, :]
        m2 = scr_rows.h[:, 2, :]
        rd = scr_rows.begin_write()
        t1 = p.op("vector", lambda e, mu=mu: e.tensor_scalar_mul(
            mu, psS.h[0:1, :], 1.0 / D), deps=rd + psS.wr)
        psS.read(t1)
        t2 = p.op("vector", lambda e, q=q: e.tensor_scalar_mul(
            q, psQ.h[0:1, :], 1.0 / D), deps=rd + psQ.wr)
        psQ.read(t2)
        t3 = p.op("vector", lambda e, mu=mu, m2=m2: e.tensor_tensor(
            out=m2, in0=mu, in1=mu, op=ALU.mult), deps=[t1])
        t4 = p.op("vector", lambda e, q=q, m2=m2: e.tensor_tensor(
            out=q, in0=q, in1=m2, op=ALU.subtract), deps=[t2, t3])
        t5 = p.op("scalar", lambda e, q=q, m2=m2: e.activation(
            out=m2, in_=q, func=AF.Ln, bias=eps_c.h[:]),
            deps=[t4] + eps_c.wr)
        t6 = p.op("scalar", lambda e, q=q, m2=m2: e.activation(
            out=q, in_=m2, func=AF.Exp, scale=-0.5), deps=[t5])
        scr_rows.wr = [t1, t3, t4, t5, t6]
        rb = scr_rows_b.begin_write()
        t7 = p.op("vector", lambda e, q=q: e.tensor_copy(
            out=scr_rows_b.h[:, 0, :], in_=q), deps=[t6] + rb)
        t8 = p.op("vector", lambda e, mu=mu: e.tensor_copy(
            out=scr_rows_b.h[:, 1, :], in_=mu), deps=[t1, t5] + rb)
        scr_rows.read(t7)
        scr_rows.read(t8)
        scr_rows_b.wr = [t7, t8]
        psR = pp.acquire()
        psM = pp.acquire()
        dR = psR.begin_write()
        dM = psM.begin_write()
        tR = p.op("tensor", lambda e: e.matmul(
            psR.h[:], ones_row_b.h[:], scr_rows_b.h[0:1, 0, :],
            start=True, stop=True), deps=dR + [t7] + ones_row_b.wr)
        scr_rows_b.read(tR)
        psR.wrote(tR)
        tM = p.op("tensor", lambda e: e.matmul(
            psM.h[:], ones_row_b.h[:], scr_rows_b.h[0:1, 1, :],
            start=True, stop=True), deps=dM + [t8] + ones_row_b.wr)
        scr_rows_b.read(tM)
        psM.wrote(tM)
        return psM, psR

    def ln_apply(xc, xc_view, psM, psR, out_write):
        lnt = lns["lnt"]
        toks = []
        for i in range(PC_D):
            par = i % 2
            lv = lnt.h[:, par, :]
            dl = lnt.begin_write() if i < 2 else list(lnt.wr) + list(lnt.rd)
            tok1 = p.op("vector", lambda e, i=i, lv=lv: e.tensor_tensor(
                out=lv, in0=xc_view(i), in1=psM.h[:], op=ALU.subtract),
                deps=dl + xc.wr + psM.wr)
            xc.read(tok1)
            psM.read(tok1)
            lnt.wrote(tok1)
            tok2 = out_write(i, lv, [tok1] + psR.wr)
            psR.read(tok2)
            lnt.read(tok2)
            toks.append(tok2)
        return toks

    def macaron_half(tag, src_dram, dst_dram, dst_tok_reg, last=False):
        xsc = mk(tag + "_xsc", [128, 8, 2048], BF16)
        xcm = mk(tag + "_xcm", [128, 8, 1024], F32)
        bw_xsc = xsc.begin_write()
        xsc_toks = []
        for oc in range(2):
            dma_load(xcm, src_dram[:, oc], deps=[dram_tok["xspA"][2 * oc],
                                                 dram_tok["xspA"][2 * oc + 1]]
                     if src_dram is xspA else ())
            for sc in range(2):
                s0 = sc * 512
                g0 = oc * 1024 + s0

                def xv(i, s0=s0):
                    return xcm.h[:, i, s0:s0 + 512]

                psM, psR = ln_512(xcm, xv)

                def ow(i, lv, deps, g0=g0, psR=psR):
                    tok = p.op("vector", lambda e, i=i, g0=g0, psR=psR,
                               lv=lv: e.tensor_tensor(
                                   out=xsc.h[:, i, g0:g0 + 512],
                                   in0=lv, in1=psR.h[:], op=ALU.mult),
                               deps=deps + bw_xsc)
                    xsc.wrote(tok)
                    return tok

                xsc_toks += ln_apply(xcm, xv, psM, psR, ow)
        xcm.free()
        if last:
            lns_free()

        hbuf = [mk(tag + f"_hbuf{i}", [128, 8, 2048], BF16)
                for i in range(3)]
        xcm2 = mk(tag + "_xcm2", [128, 8, 512], F32)
        wg_l = [mk(tag + f"_wgl{i}", [128, 8, 256], BF16) for i in range(2)]
        wu_l = [mk(tag + f"_wul{i}", [128, 8, 256], BF16) for i in range(2)]
        sg_t = mk(tag + "_sg", [128, 512], BF16)
        bw_h = hbuf[0].begin_write() + hbuf[1].begin_write() \
            + hbuf[2].begin_write()
        for sl in range(12):
            wgb = wg_l[sl % 2]
            wub = wu_l[sl % 2]
            dma_load(wgb, wg[:, sl], cast=True)
            dma_load(wub, wu[:, sl], cast=True)
            for mb in range(2):
                m = sl * 2 + mb
                for tc in range(4):
                    c0 = tc * 512
                    psG = pp.acquire()
                    psU = pp.acquire()
                    dG = psG.begin_write() + wgb.wr + xsc.wr
                    dU = psU.begin_write() + wub.wr + xsc.wr
                    tg = tu = None
                    for kc in range(PC_D):
                        tg = p.op("tensor", lambda e, kc=kc, mb=mb, c0=c0,
                                  wgb=wgb, psG=psG: e.matmul(
                                      psG.h[:],
                                      wgb.h[:, kc, mb * 128:(mb + 1) * 128],
                                      xsc.h[:, kc, c0:c0 + 512],
                                      start=(kc == 0), stop=(kc == PC_D - 1)),
                                  deps=dG)
                        wgb.read(tg)
                        xsc.read(tg)
                        tu = p.op("tensor", lambda e, kc=kc, mb=mb, c0=c0,
                                  wub=wub, psU=psU: e.matmul(
                                      psU.h[:],
                                      wub.h[:, kc, mb * 128:(mb + 1) * 128],
                                      xsc.h[:, kc, c0:c0 + 512],
                                      start=(kc == 0), stop=(kc == PC_D - 1)),
                                  deps=dU)
                        wub.read(tu)
                        xsc.read(tu)
                    psG.wrote(tg)
                    psU.wrote(tu)
                    dsg = sg_t.begin_write()
                    ts_ = p.op("scalar", lambda e, psG=psG: e.activation(
                        out=sg_t.h[:], in_=psG.h[:], func=AF.Silu,
                        scale=qsc_c.h[:]), deps=dsg + psG.wr + qsc_c.wr)
                    psG.read(ts_)
                    sg_t.wrote(ts_)
                    hpc = hbuf[m // 8]
                    th = p.op("vector", lambda e, m=m, c0=c0, psU=psU,
                              hpc=hpc: e.tensor_tensor(
                                  out=hpc.h[:, m % 8, c0:c0 + 512],
                                  in0=sg_t.h[:], in1=psU.h[:], op=ALU.mult),
                              deps=[ts_] + psU.wr + bw_h)
                    psU.read(th)
                    sg_t.read(th)
                    hpc.wrote(th)
        xsc.free()
        for b in wg_l + wu_l:
            b.free()
        sg_t.free()

        wdres = [mk(tag + f"_wdres{i}", [128, 4, 1024], BF16)
                 for i in range(6)]
        for i, wpc in enumerate(wdres):
            dma_load(wpc, wd[:, 4 * i:4 * i + 4, :], cast=True)
        stores = []
        for oc in range(2):
            for tc2 in range(2):
                tb = oc * 2 + tc2
                c0 = tb * 512
                dma_load(xcm2, src_dram[:, oc, :, tc2 * 512:tc2 * 512 + 512],
                         deps=[dram_tok["xspA"][tb]]
                         if src_dram is xspA else ())
                ev = []
                for dg in range(2):
                    pses = [pp.acquire() for _ in range(4)]
                    wdtoks = [t for w in wdres for t in w.wr]
                    dps = [ps.begin_write() + hbuf[0].wr + hbuf[1].wr
                           + hbuf[2].wr + wdtoks for ps in pses]
                    lasts = [None] * 4
                    for k in range(24):
                        wpc = wdres[k // 4]
                        kk = k % 4
                        hpc = hbuf[k // 8]
                        for j in range(4):
                            do = dg * 4 + j
                            ps = pses[j]
                            tok = p.op("tensor", lambda e, kk=kk, do=do,
                                       c0=c0, k=k, ps=ps, wpc=wpc, hpc=hpc:
                                       e.matmul(
                                           ps.h[:],
                                           wpc.h[:, kk, do * 128:(do + 1) * 128],
                                           hpc.h[:, k % 8, c0:c0 + 512],
                                           start=(k == 0), stop=(k == 23)),
                                       deps=dps[j])
                            hpc.read(tok)
                            wpc.read(tok)
                            lasts[j] = tok
                    for j in range(4):
                        do = dg * 4 + j
                        ps = pses[j]
                        ps.wrote(lasts[j])
                        tok = p.op("vector", lambda e, do=do, ps=ps:
                                   e.scalar_tensor_tensor(
                                       out=xcm2.h[:, do, :], in0=ps.h[:],
                                       scalar=0.5 * QSC * QSC,
                                       in1=xcm2.h[:, do, :],
                                       op0=ALU.mult, op1=ALU.add),
                                   deps=ps.wr + xcm2.wr)
                        ps.read(tok)
                        xcm2.wrote(tok)
                        ev.append(tok)
                st = dma_store(dst_dram[:, oc, :, tc2 * 512:tc2 * 512 + 512],
                               xcm2, deps=ev)
                if dst_tok_reg is not None:
                    dst_tok_reg[tb].append(st)
                stores.append(st)
        for b in hbuf + wdres + [xcm2]:
            b.free()
        return stores

    def phase_h2():
        xc = mk("p3_xc", [128, 8, 512], F32)
        hl = mk("p3_hl", [128, 8, 512], F32)
        h2c = mk("p3_h2c", [128, 8, 512], BF16)
        lnt2 = mk("p3_lnt2", [128, 512], F32)
        for ci in range(4):
            oc, tc2 = divmod(ci, 2)
            dma_load(xc, xspA[:, oc, :, tc2 * 512:tc2 * 512 + 512],
                     deps=[dram_tok["xspA"][ci]])

            def xv(i):
                return xc.h[:, i, :]

            psM, psR = ln_512(xc, xv)
            bw_hl = hl.begin_write()

            def ow(i, lv, deps, psR=psR):
                tok = p.op("vector", lambda e, i=i, psR=psR, lv=lv:
                           e.tensor_tensor(
                    out=hl.h[:, i, :], in0=lv, in1=psR.h[:],
                    op=ALU.mult), deps=deps + bw_hl)
                hl.wrote(tok)
                return tok

            ln_apply(xc, xv, psM, psR, ow)

            def xv2(i):
                return hl.h[:, i, :]

            psM2, psR2 = ln_512(hl, xv2)
            bw_h2 = h2c.begin_write()

            def ow2(i, lv, deps, psR2=psR2):
                d2 = lnt2.begin_write()
                tok2 = p.op("vector", lambda e, i=i, psR2=psR2, lv=lv:
                            e.tensor_tensor(out=lnt2.h[:], in0=lv,
                                            in1=psR2.h[:], op=ALU.mult),
                            deps=deps + d2)
                lnt2.wrote(tok2)
                tok3 = p.op("vector", lambda e, i=i: e.tensor_scalar(
                    out=h2c.h[:, i, :], in0=lnt2.h[:],
                    scalar1=ng_sb.h[:, i:i + 1], scalar2=nb_sb.h[:, i:i + 1],
                    op0=ALU.mult, op1=ALU.add),
                    deps=[tok2] + bw_h2 + ng_sb.wr + nb_sb.wr)
                lnt2.read(tok3)
                h2c.wrote(tok3)
                return tok3

            ln_apply(hl, xv2, psM2, psR2, ow2)
            dram_tok["h2sp"][ci].append(dma_store(h2sp[:, ci], h2c))
            dram_tok["xasp"][ci].append(dma_store(xasp[:, ci], hl))
        for b in (xc, hl, h2c, lnt2):
            b.free()

    def mamba_dir(tag, widx, rev):
        TCS = 512
        if widx == 2:
            for d_ in (conv_sb, cb_sb, dtb_sb, dp_sb, A_sb, xpd_sb,
                       xpb_sb, xpc_sb, dtw_sb):
                if 1 in d_:
                    d_.pop(1).free()
        xsraw = mk(tag + "_xsraw", [128, 16, TCS + 3], BF16)
        xs_c = mk(tag + "_xs", [128, 16, TCS], BF16)
        sz_p = [mk(tag + f"_sz{i}", [128, 16, TCS], BF16) for i in range(2)]
        dtx = mk(tag + "_dtx", [128, 16, TCS], BF16)
        hcar = mk(tag + "_hcar", [128, 32, 8, 1], F32)
        zc = mk(tag + "_zc", [128, 1], F32)
        f_t = mk(tag + "_ft", [128, 8, 1], BF16)
        dtr16 = mk(tag + "_dtr", [DTR, TCS], BF16)
        Brow = mk(tag + "_Brow", [DST, TCS], BF16)
        Crow = mk(tag + "_Crow", [DST, TCS], BF16)
        ysum = mk(tag + "_ys", [128, TCS], BF16)
        ytmp = mk(tag + "_yt", [128, TCS], BF16)
        spt = [mk(tag + f"_sp{i}", [128, TCS], BF16) for i in range(2)]
        rtmp = mk(tag + "_rt", [128, 4, 128], BF16)
        ctmp = mk(tag + "_ct", [128, TCS], BF16)
        dt_t = mk(tag + "_dt", [128, 16, TCS], BF16)
        Bball = mk(tag + "_Bb", [128, DST, TCS], BF16)
        Cball = mk(tag + "_Cb", [128, DST, TCS], BF16)
        dA3p = [mk(tag + f"_dA{i}", [128, 8, TCS], BF16) for i in range(2)]
        dB3 = mk(tag + "_dB", [128, 8, TCS], BF16)
        wil = [mk(tag + f"_wil{i}", [128, 8, 256], BF16)
               for i in range(2)]
        wol = mk(tag + "_wol", [128, 16, 128], BF16)
        st_f = [mk(tag + f"_stf{i}", [128, TCS], BF16) for i in range(2)]
        if rev:
            st_x = [mk(tag + "_stx0", [128, TCS], BF16)]
            st_m = [mk(tag + "_stm0", [128, TCS], BF16)]
            st_g = [mk(tag + "_stg0", [128, TCS], BF16)]
        else:
            st_x = st_m = st_g = []

        zt = p.op("vector", lambda e: e.memset(hcar.h[:], 0.0),
                  deps=hcar.begin_write())
        hcar.wrote(zt)
        ztz = p.op("vector", lambda e: e.memset(zc.h[:], 0.0),
                   deps=zc.begin_write())
        zc.wrote(ztz)
        zt2 = p.op("vector", lambda e: e.memset(xsraw.h[:, :, 0:3], 0.0),
                   deps=xsraw.begin_write())
        xsraw.wrote(zt2)
        state = {"tail": [zt2], "unit": 0}

        conv_w = conv_sb[widx]
        cb_w = cb_sb[widx]
        dtb_w = dtb_sb[widx]
        dp_w = dp_sb[widx]
        A_w = A_sb[widx]
        xpd_w = xpd_sb[widx]
        xpb_w = xpb_sb[widx]
        xpc_w = xpc_sb[widx]
        dtw_w = dtw_sb[widx]

        def prep_a(gc):
            raise RuntimeError("unused")

        def prep_gen(gc):
            sz_c = sz_p[gc % 2]
            hcm = mk(tag + f"_hcm{gc}", [128, 8, TCS], BF16)
            if not rev:
                dma_load(hcm, h2sp[:, gc], deps=[dram_tok["h2sp"][gc]])
            else:
                ci = 3 - gc
                bw_hcm = hcm.begin_write()
                for j in range(4):
                    src0 = 512 - (j + 1) * 128
                    tok = p.dma("sync", lambda e, j=j, src0=src0, hcm=hcm:
                                e.dma_start(
                                    out=hcm.h[:, :, j * 128:(j + 1) * 128],
                                    in_=h2sp[:, ci, :, src0:src0 + 128]),
                                deps=bw_hcm + _flat([dram_tok["h2sp"][ci]]))
                    hcm.wrote(tok)
                slot_tok = [{"wr": [], "rd": []} for _ in range(4)]
                rslot_tok = [{"wr": [], "rd": []} for _ in range(4)]
                bT = pp.bT
                for ti in range(32):
                    j, pc = divmod(ti, PC_D)
                    sl = ti % 4
                    stk = slot_tok[sl]
                    rtk = rslot_tok[sl]
                    dT = stk["wr"] + stk["rd"] + hcm.wr + eye_sb.wr
                    tk1 = p.op("tensor", lambda e, pc=pc, j=j, sl=sl,
                               hcm=hcm: e.transpose(
                                   bT.h[:, sl * 128:(sl + 1) * 128],
                                   hcm.h[:, pc, j * 128:(j + 1) * 128],
                                   eye_sb.h[:]), deps=dT)
                    hcm.read(tk1)
                    eye_sb.read(tk1)
                    stk["wr"] = [tk1]
                    stk["rd"] = []
                    tk2 = p.op("scalar", lambda e, sl=sl: e.activation(
                        out=rtmp.h[:, sl, :],
                        in_=bT.h[:, sl * 128:(sl + 1) * 128],
                        func=AF.Copy),
                        deps=[tk1] + rtk["wr"] + rtk["rd"])
                    stk["rd"] = [tk2]
                    rtk["wr"] = [tk2]
                    rtk["rd"] = []
                    psF = pp.acquire()
                    dF = psF.begin_write() + [tk2] + jmat_sb.wr
                    tk3 = p.op("tensor", lambda e, sl=sl, psF=psF: e.matmul(
                        psF.h[:, 0:128], rtmp.h[:, sl, :], jmat_sb.h[:],
                        start=True, stop=True), deps=dF)
                    rtk["rd"] = [tk3]
                    jmat_sb.read(tk3)
                    psF.wrote(tk3)
                    tk4 = p.op("scalar", lambda e, pc=pc, j=j, psF=psF,
                               hcm=hcm: e.activation(
                                   out=hcm.h[:, pc, j * 128:(j + 1) * 128],
                                   in_=psF.h[:, 0:128], func=AF.Copy),
                               deps=[tk3, tk1] + hcm.wr)
                    psF.read(tk4)
                    hcm.wrote(tk4)

            yield
            bw_xsm = xsraw.rd + xsraw.wr
            xsraw.rd = []
            xsraw.wr = list(state["tail"])
            bw_sz = sz_c.begin_write()
            state["xsraw_tok"] = [None] * PC_DI
            for sl in range(16):
                wb = wil[sl % 2]
                dma_load(wb, inw[widx][:, sl], cast=True)
                for mb in range(2):
                    m = sl * 2 + mb
                    ps = pp.acquire()
                    dmm = ps.begin_write() + wb.wr + hcm.wr
                    tok = None
                    for kc in range(PC_D):
                        tok = p.op("tensor", lambda e, kc=kc, mb=mb,
                                   ps=ps, hcm=hcm, wb=wb: e.matmul(
                                       ps.h[:],
                                       wb.h[:, kc, mb * 128:(mb + 1) * 128],
                                       hcm.h[:, kc, :],
                                       start=(kc == 0),
                                       stop=(kc == PC_D - 1)), deps=dmm)
                        wb.read(tok)
                        hcm.read(tok)
                    ps.wrote(tok)
                    if m < PC_DI:
                        tok2 = p.op("scalar", lambda e, m=m, ps=ps:
                                    e.activation(
                                        out=xsraw.h[:, m, 3:3 + TCS],
                                        in_=ps.h[:], func=AF.Copy),
                                    deps=ps.wr + bw_xsm)
                        xsraw.wrote(tok2)
                        state["xsraw_tok"][m] = tok2
                    else:
                        tok2 = p.op("scalar", lambda e, m=m, ps=ps,
                                    sz_c=sz_c: e.activation(
                                        out=sz_c.h[:, m - PC_DI, :],
                                        in_=ps.h[:], func=AF.Silu),
                                    deps=ps.wr + bw_sz)
                        sz_c.wrote(tok2)
                    ps.read(tok2)
                yield
            hcm.free()

            for dc in range(PC_DI):
                d0 = ctmp.begin_write() + [state["xsraw_tok"][dc]] \
                    + state["tail"] + conv_w.wr
                tk = p.op("vector", lambda e, dc=dc: e.tensor_scalar_mul(
                    ctmp.h[:], xsraw.h[:, dc, 0:TCS],
                    conv_w.h[:, dc, 0:1]), deps=d0)
                xsraw.read(tk)
                conv_w.read(tk)
                ctmp.wrote(tk)
                for kk in range(1, 4):
                    tk = p.op("vector", lambda e, dc=dc, kk=kk:
                              e.scalar_tensor_tensor(
                                  out=ctmp.h[:],
                                  in0=xsraw.h[:, dc, kk:kk + TCS],
                                  scalar=conv_w.h[:, dc, kk:kk + 1],
                                  in1=ctmp.h[:], op0=ALU.mult, op1=ALU.add),
                              deps=[tk])
                    xsraw.read(tk)
                    ctmp.wr = [tk]
                tk2 = p.op("scalar", lambda e, dc=dc: e.activation(
                    out=xs_c.h[:, dc, :], in_=ctmp.h[:], func=AF.Silu,
                    bias=cb_w.h[:, dc:dc + 1]),
                    deps=[tk] + list(xs_c.rd) + list(xs_c.wr) + cb_w.wr)
                ctmp.read(tk2)
                xs_c.wrote(tk2)
                yield
            ttail = p.op("vector", lambda e: e.tensor_copy(
                out=xsraw.h[:, :, 0:3], in_=xsraw.h[:, :, TCS:TCS + 3]),
                deps=xsraw.rd + xsraw.wr)
            state["tail"] = [ttail]
            xsraw.wrote(ttail)

            psD = pp.acquire(); psB_ = pp.acquire(); psC_ = pp.acquire()
            dD = psD.begin_write() + xpd_w.wr
            dB = psB_.begin_write() + xpb_w.wr
            dC = psC_.begin_write() + xpc_w.wr
            tD = tB = tC = None
            for kc in range(PC_DI):
                tD = p.op("tensor", lambda e, kc=kc, psD=psD: e.matmul(
                    psD.h[0:DTR, :], xpd_w.h[:, kc, :], xs_c.h[:, kc, :],
                    start=(kc == 0), stop=(kc == PC_DI - 1)),
                    deps=dD + xs_c.wr)
                xs_c.read(tD)
                xpd_w.read(tD)
                tB = p.op("tensor", lambda e, kc=kc, psB_=psB_: e.matmul(
                    psB_.h[0:DST, :], xpb_w.h[:, kc, :], xs_c.h[:, kc, :],
                    start=(kc == 0), stop=(kc == PC_DI - 1)),
                    deps=dB + xs_c.wr)
                xs_c.read(tB)
                xpb_w.read(tB)
                tC = p.op("tensor", lambda e, kc=kc, psC_=psC_: e.matmul(
                    psC_.h[0:DST, :], xpc_w.h[:, kc, :], xs_c.h[:, kc, :],
                    start=(kc == 0), stop=(kc == PC_DI - 1)),
                    deps=dC + xs_c.wr)
                xs_c.read(tC)
                xpc_w.read(tC)
            psD.wrote(tD); psB_.wrote(tB); psC_.wrote(tC)
            e1 = p.op("scalar", lambda e, psD=psD: e.activation(
                out=dtr16.h[:], in_=psD.h[0:DTR, :], func=AF.Copy),
                deps=psD.wr + dtr16.begin_write())
            psD.read(e1); dtr16.wrote(e1)
            e2 = p.op("scalar", lambda e, psB_=psB_: e.activation(
                out=Brow.h[:], in_=psB_.h[0:DST, :], func=AF.Copy),
                deps=psB_.wr + Brow.begin_write())
            psB_.read(e2); Brow.wrote(e2)
            e3 = p.op("scalar", lambda e, psC_=psC_: e.activation(
                out=Crow.h[:], in_=psC_.h[0:DST, :], func=AF.Copy),
                deps=psC_.wr + Crow.begin_write())
            psC_.read(e3); Crow.wrote(e3)

            bw_B = Bball.begin_write()
            bw_C = Cball.begin_write()
            for s in range(DST):
                for (row, ball, bw) in ((Brow, Bball, bw_B),
                                        (Crow, Cball, bw_C)):
                    psbc = pp.acquire()
                    dbc = psbc.begin_write() + row.wr + sel_sb.wr
                    tk = p.op("tensor", lambda e, s=s, row=row, psbc=psbc:
                              e.matmul(psbc.h[:], sel_sb.h[:, s, :],
                                       row.h[:], start=True, stop=True),
                              deps=dbc)
                    row.read(tk)
                    sel_sb.read(tk)
                    psbc.wrote(tk)
                    tk2 = p.op("scalar", lambda e, s=s, ball=ball,
                               psbc=psbc: e.activation(
                                   out=ball.h[:, s, :], in_=psbc.h[:],
                                   func=AF.Copy), deps=psbc.wr + bw)
                    psbc.read(tk2)
                    ball.wrote(tk2)

        def prep_b(gc):
            bw_dt = dt_t.begin_write()
            state["dt_tok"] = [None] * PC_DI
            state["dtx_tok"] = [None] * PC_DI
            for dc in range(PC_DI):
                psdt = pp.acquire()
                ddt = psdt.begin_write() + dtr16.wr + dtw_w.wr
                tk = p.op("tensor", lambda e, dc=dc, psdt=psdt: e.matmul(
                    psdt.h[:], dtw_w.h[:, dc, :], dtr16.h[:],
                    start=True, stop=True), deps=ddt)
                dtr16.read(tk)
                dtw_w.read(tk)
                psdt.wrote(tk)
                sp1, sp2 = spt
                a1 = p.op("scalar", lambda e, dc=dc, psdt=psdt, sp1=sp1:
                          e.activation(out=sp1.h[:], in_=psdt.h[:],
                                       func=AF.Abs,
                                       bias=dtb_w.h[:, dc:dc + 1]),
                          deps=psdt.wr + sp1.begin_write() + dtb_w.wr)
                psdt.read(a1); sp1.wrote(a1)
                a2 = p.op("scalar", lambda e, sp1=sp1, sp2=sp2:
                          e.activation(out=sp2.h[:], in_=sp1.h[:],
                                       func=AF.Exp, scale=-1.0),
                          deps=[a1] + sp2.begin_write())
                sp1.read(a2); sp2.wrote(a2)
                a3 = p.op("scalar", lambda e, sp1=sp1, sp2=sp2:
                          e.activation(out=sp1.h[:], in_=sp2.h[:],
                                       func=AF.Ln, bias=one_c.h[:]),
                          deps=[a2, a1] + one_c.wr)
                sp2.read(a3); sp1.wr = [a3]; sp1.rd = []
                a4 = p.op("scalar", lambda e, dc=dc, psdt=psdt, sp2=sp2:
                          e.activation(out=sp2.h[:], in_=psdt.h[:],
                                       func=AF.Relu,
                                       bias=dtb_w.h[:, dc:dc + 1]),
                          deps=psdt.wr + [a3] + dtb_w.wr)
                psdt.read(a4); sp2.wr = [a4]; sp2.rd = []
                a5 = p.op("vector", lambda e, dc=dc, sp1=sp1, sp2=sp2:
                          e.tensor_tensor(out=dt_t.h[:, dc, :],
                                          in0=sp1.h[:], in1=sp2.h[:],
                                          op=ALU.add),
                          deps=[a3, a4] + bw_dt)
                sp1.read(a5); sp2.read(a5)
                dt_t.wrote(a5)
                state["dt_tok"][dc] = a5
                a6 = p.op("vector", lambda e, dc=dc:
                          e.tensor_tensor(out=dtx.h[:, dc, :],
                                          in0=dt_t.h[:, dc, :],
                                          in1=xs_c.h[:, dc, :],
                                          op=ALU.mult),
                          deps=[a5] + xs_c.wr + dtx.rd + dtx.wr)
                xs_c.read(a6)
                dt_t.read(a6)
                dtx.rd = []
                dtx.wrote(a6)
                state["dtx_tok"][dc] = a6

        def scan_chunk(gc, gen=None):
            sz_c = sz_p[gc % 2]
            ybuf_toks = []
            for dc in range(PC_DI):
                ysum_tok = None
                for g in range(2):
                    s0 = g * 8
                    dA3 = dA3p[state["unit"] % 2]
                    state["unit"] += 1
                    bw_dA = dA3.begin_write()
                    exp_toks = []
                    for j in range(8):
                        s = s0 + j
                        tk = p.op("scalar", lambda e, dc=dc, s=s, j=j,
                                  dA3=dA3: e.activation(
                                      out=dA3.h[:, j, :],
                                      in_=dt_t.h[:, dc, :], func=AF.Exp,
                                      scale=A_w.h[:, dc, s:s + 1]),
                                  deps=bw_dA + [state["dt_tok"][dc]]
                                  + A_w.wr)
                        dt_t.read(tk)
                        A_w.read(tk)
                        dA3.wrote(tk)
                        exp_toks.append(tk)
                    bw_dB = dB3.begin_write()
                    tB_toks = []
                    for j in range(8):
                        s = s0 + j
                        tk = p.op("vector", lambda e, dc=dc, s=s, j=j:
                                  e.tensor_tensor(
                                      out=dB3.h[:, j, :],
                                      in0=dtx.h[:, dc, :],
                                      in1=Bball.h[:, s, :], op=ALU.mult),
                                  deps=bw_dB + [state["dtx_tok"][dc]]
                                  + Bball.wr)
                        dtx.read(tk)
                        Bball.read(tk)
                        tB_toks.append(tk)
                    dB3.wr = list(tB_toks); dB3.rd = []
                    blk = dc * 2 + g
                    f1 = p.op("vector", lambda e, blk=blk, dA3=dA3:
                              e.tensor_tensor(
                                  out=f_t.h[:], in0=dA3.h[:, :, 0:1],
                                  in1=hcar.h[:, blk, :, :], op=ALU.mult),
                              deps=exp_toks + hcar.wr + f_t.begin_write())
                    f_t.wrote(f1)
                    hcar.read(f1)
                    f2 = p.op("vector", lambda e: e.tensor_tensor(
                        out=dB3.h[:, :, 0:1], in0=dB3.h[:, :, 0:1],
                        in1=f_t.h[:], op=ALU.add),
                        deps=[f1] + tB_toks)
                    f_t.read(f2)
                    dB3.wr = dB3.wr + [f2]
                    f3 = p.op("vector", lambda e, dA3=dA3: e.memset(
                        dA3.h[:, :, 0:1], 0.0), deps=[f1])
                    dA3.wrote(f3)
                    th = p.op("vector", lambda e, dA3=dA3:
                              e.tensor_tensor_scan(
                                  out=dA3.h[:].rearrange("p a b -> p (a b)"),
                                  data0=dA3.h[:].rearrange("p a b -> p (a b)"),
                                  data1=dB3.h[:].rearrange("p a b -> p (a b)"),
                                  initial=zc.h[:], op0=ALU.mult,
                                  op1=ALU.add),
                              deps=exp_toks + [f2, f3] + zc.wr
                              + dB3.wr)
                    dA3.read(th)
                    dA3.wrote(th)
                    dB3.read(th)
                    zc.read(th)
                    scan_toks = [th]
                    tcr = p.op("vector", lambda e, blk=blk, dA3=dA3:
                               e.tensor_copy(
                                   out=hcar.h[:, blk, :, :],
                                   in_=dA3.h[:, :, TCS - 1:TCS]),
                               deps=scan_toks + hcar.rd + hcar.wr)
                    dA3.read(tcr)
                    hcar.wr = [tcr]
                    hcar.rd = []
                    tpr = p.op("vector", lambda e, s0=s0, dA3=dA3:
                               e.tensor_tensor(
                                   out=dB3.h[:], in0=dA3.h[:],
                                   in1=Cball.h[:, s0:s0 + 8, :],
                                   op=ALU.mult),
                               deps=scan_toks + dB3.rd + dB3.wr
                               + Cball.wr + [tcr])
                    Cball.read(tpr)
                    dA3.read(tpr)
                    dB3.wr = [tpr]; dB3.rd = []
                    for (w0, w1) in ((4, 4), (2, 2), (1, 1)):
                        tk = p.op("vector", lambda e, w0=w0:
                                  e.tensor_tensor(
                                      out=dB3.h[:, 0:w0, :],
                                      in0=dB3.h[:, 0:w0, :],
                                      in1=dB3.h[:, w0:2 * w0, :],
                                      op=ALU.add),
                                  deps=dB3.wr)
                        dB3.wr = dB3.wr + [tk]
                    if g == 0:
                        ysum_tok = p.op("vector", lambda e: e.tensor_copy(
                            out=ysum.h[:], in_=dB3.h[:, 0, :]),
                            deps=dB3.wr + ysum.begin_write())
                    else:
                        ysum_tok = p.op("vector", lambda e: e.tensor_tensor(
                            out=ysum.h[:], in0=ysum.h[:],
                            in1=dB3.h[:, 0, :], op=ALU.add),
                            deps=dB3.wr + [ysum_tok])
                    dB3.read(ysum_tok)
                    ysum.wrote(ysum_tok)
                t1 = p.op("vector", lambda e, dc=dc:
                          e.scalar_tensor_tensor(
                              out=ytmp.h[:], in0=xs_c.h[:, dc, :],
                              scalar=dp_w.h[:, dc:dc + 1], in1=ysum.h[:],
                              op0=ALU.mult, op1=ALU.add),
                          deps=[ysum_tok] + xs_c.wr + ytmp.begin_write()
                          + dp_w.wr)
                xs_c.read(t1)
                ysum.read(t1)
                dp_w.read(t1)
                ytmp.wrote(t1)
                t2 = p.op("vector", lambda e, dc=dc, sz_c=sz_c:
                          e.tensor_tensor(
                              out=dtx.h[:, dc, :], in0=ytmp.h[:],
                              in1=sz_c.h[:, dc, :], op=ALU.mult),
                          deps=[t1] + sz_c.wr + dtx.rd + dtx.wr)
                ytmp.read(t2)
                sz_c.read(t2)
                dtx.wrote(t2)
                ybuf_toks.append(t2)
                if gen is not None:
                    next(gen, None)
                    next(gen, None)
            state["ybuf_toks"] = ybuf_toks

        def out_chunk(gc):
            ybuf_toks = state["ybuf_toks"]
            ev = []
            for oc8 in range(PC_D):
                dma_load(wol, outw[widx][:, oc8], cast=True)
                pso = pp.acquire()
                dmo = pso.begin_write() + wol.wr
                tok = None
                for kc in range(PC_DI):
                    tok = p.op("tensor", lambda e, kc=kc, pso=pso: e.matmul(
                        pso.h[:], wol.h[:, kc, :], dtx.h[:, kc, :],
                        start=(kc == 0), stop=(kc == PC_DI - 1)),
                        deps=dmo + [ybuf_toks[kc]])
                    wol.read(tok)
                    dtx.read(tok)
                pso.wrote(tok)
                par = oc8 % 2
                if not rev:
                    fo_s = st_f[par]
                    tk = p.op("scalar", lambda e, pso=pso, fo_s=fo_s:
                              e.activation(out=fo_s.h[:], in_=pso.h[:],
                                           func=AF.Copy),
                              deps=pso.wr + fo_s.begin_write())
                    pso.read(tk)
                    fo_s.wrote(tk)
                    st = dma_store(fwdsp[:, gc, oc8, :], fo_s)
                    dram_tok["fwdsp"][gc].append(st)
                else:
                    foc_s, xc4_s = st_f[par], st_x[0]
                    mtmp_s, gt_s = st_m[0], st_g[0]
                    dma_load(foc_s, fwdsp[:, gc, oc8, :],
                             deps=[dram_tok["fwdsp"][gc]])
                    dma_load(xc4_s, xasp[:, gc, oc8, :], cast=True)
                    m1_ = p.op("vector", lambda e, pso=pso, foc_s=foc_s,
                               mtmp_s=mtmp_s: e.tensor_tensor(
                                   out=mtmp_s.h[:], in0=pso.h[:],
                                   in1=foc_s.h[:], op=ALU.add),
                               deps=pso.wr + foc_s.wr
                               + mtmp_s.begin_write())
                    pso.read(m1_)
                    foc_s.read(m1_)
                    mtmp_s.wrote(m1_)
                    m2_ = p.op("vector", lambda e, mtmp_s=mtmp_s,
                               xc4_s=xc4_s: e.tensor_tensor(
                                   out=mtmp_s.h[:], in0=mtmp_s.h[:],
                                   in1=xc4_s.h[:], op=ALU.add),
                               deps=[m1_] + xc4_s.wr)
                    xc4_s.read(m2_)
                    mtmp_s.wr = [m2_]
                    m3_ = p.op("scalar", lambda e, gt_s=gt_s, mtmp_s=mtmp_s:
                               e.activation(out=gt_s.h[:], in_=mtmp_s.h[:],
                                            func=AF.Gelu),
                               deps=[m2_] + gt_s.begin_write())
                    mtmp_s.read(m3_)
                    gt_s.wrote(m3_)
                    m4_ = p.op("vector", lambda e, xc4_s=xc4_s, gt_s=gt_s,
                               mtmp_s=mtmp_s: e.tensor_tensor(
                                   out=mtmp_s.h[:], in0=xc4_s.h[:],
                                   in1=gt_s.h[:], op=ALU.add),
                               deps=[m3_] + mtmp_s.wr + xc4_s.wr)
                    gt_s.read(m4_)
                    xc4_s.read(m4_)
                    mtmp_s.wrote(m4_)
                    st = dma_store(xspB[:, gc, oc8, :], mtmp_s,
                                   eng="gpsimd")
                    dram_tok["xspB"][gc].append(st)
                ev.append(st)
            return ev

        g0 = prep_gen(0)
        for _ in g0:
            pass
        prep_b(0)
        for gc in range(4):
            gen = prep_gen(gc + 1) if gc + 1 < 4 else None
            scan_chunk(gc, gen)
            if gen is not None:
                for _ in gen:
                    pass
            out_chunk(gc)
            if gc + 1 < 4:
                prep_b(gc + 1)

        for b in (xs_c, sz_p[0], sz_p[1], xsraw, hcar, zc, f_t, dtx,
                  dtr16, Brow, Crow, ysum, ytmp, rtmp, ctmp, dt_t, Bball,
                  Cball, dA3p[0], dA3p[1], dB3, wil[0], wil[1], wol,
                  *st_f, *st_x, *st_m, *st_g):
            b.free()


    def phase_ffn():
        xc5 = mk("p6_xc", [128, 8, 1024], F32)
        xn5 = mk("p6_xn", [128, 8, 1024], BF16)
        for oc in range(2):
            bw_x5 = xc5.begin_write()
            for half in range(2):
                ci = 2 * oc + half
                dma_load_into(xc5, xc5.h[:, :, half * 512:half * 512 + 512],
                              xspB[:, ci],
                              deps=[dram_tok["xspB"][ci]] + bw_x5)
            bw_xn = xn5.begin_write()
            for sc in range(2):
                s0 = sc * 512

                def xv(i, s0=s0):
                    return xc5.h[:, i, s0:s0 + 512]

                psM, psR = ln_512(xc5, xv)
                lnt2b = None

                def ow(i, lv, deps, s0=s0, psR=psR):
                    tok2 = p.op("vector", lambda e, i=i, s0=s0, psR=psR,
                                lv=lv: e.tensor_tensor(out=lv, in0=lv,
                                                       in1=psR.h[:],
                                                       op=ALU.mult),
                                deps=deps)
                    lns["lnt"].wrote(tok2)
                    tok3 = p.op("vector", lambda e, i=i, s0=s0, lv=lv:
                                e.tensor_scalar(
                                    out=xn5.h[:, i, s0:s0 + 512],
                                    in0=lv,
                                    scalar1=fg_sb.h[:, i:i + 1],
                                    scalar2=fb_sb.h[:, i:i + 1],
                                    op0=ALU.mult, op1=ALU.add),
                                deps=[tok2] + bw_xn + fg_sb.wr + fb_sb.wr)
                    lns["lnt"].read(tok3)
                    xn5.wrote(tok3)
                    return tok3

                ln_apply(xc5, xv, psM, psR, ow)
            x1t = mk("p6_x1t", [128, 8, 1024], F32)
            dma_load(x1t, xspA[:, oc], deps=[dram_tok["xspA"][2 * oc],
                                             dram_tok["xspA"][2 * oc + 1]])
            tadd = p.op("vector", lambda e, x1t=x1t: e.tensor_tensor(
                out=xc5.h[:], in0=xc5.h[:], in1=x1t.h[:], op=ALU.add),
                deps=xc5.wr + xc5.rd + x1t.wr)
            x1t.read(tadd)
            xc5.wr = xc5.wr + [tadd]
            x1t.free()
            hfb = mk(f"p6_hfb{oc}", [128, 32, 1024], BF16)
            f1l = [mk(f"p6_f1l{oc}_{i}", [128, 8, 512], BF16)
                   for i in range(2)]
            bw_h = hfb.begin_write()
            for sl in range(8):
                fl = f1l[sl % 2]
                dma_load(fl, ff1[:, sl], cast=True)
                for mb in range(4):
                    m = sl * 4 + mb
                    for tc in range(2):
                        c0 = tc * 512
                        ps = pp.acquire()
                        dmm = ps.begin_write() + fl.wr + xn5.wr
                        tok = None
                        for kc in range(PC_D):
                            tok = p.op("tensor", lambda e, kc=kc, mb=mb,
                                       c0=c0, fl=fl, ps=ps: e.matmul(
                                           ps.h[:],
                                           fl.h[:, kc, mb * 128:(mb + 1) * 128],
                                           xn5.h[:, kc, c0:c0 + 512],
                                           start=(kc == 0),
                                           stop=(kc == PC_D - 1)), deps=dmm)
                            fl.read(tok)
                            xn5.read(tok)
                        ps.wrote(tok)
                        tk = p.op("scalar", lambda e, m=m, c0=c0, ps=ps,
                                  hfb=hfb: e.activation(
                                      out=hfb.h[:, m, c0:c0 + 512],
                                      in_=ps.h[:], func=AF.Gelu,
                                      bias=f1b_sb.h[:, m:m + 1]),
                                  deps=ps.wr + bw_h + f1b_sb.wr)
                        ps.read(tk)
                        hfb.wrote(tk)
            for fl in f1l:
                fl.free()
            ev = []
            for dg in range(2):
                wf2h = [mk(f"p6_wf2h{oc}_{dg}_{i}", [128, 16, 512], BF16)
                        for i in range(2)]
                for i, wpc in enumerate(wf2h):
                    dw2 = wpc.begin_write()
                    for piece in range(2):
                        k0 = piece * 8
                        tok = p.dma("gpsimd", lambda e, k0=k0, dg=dg, i=i,
                                    wpc=wpc: e.dma_start(
                                        out=wpc.h[:, k0:k0 + 8, :],
                                        in_=ff2[:, dg, i * 16 + k0:
                                                i * 16 + k0 + 8, :]),
                                    deps=dw2)
                        wpc.wrote(tok)
                for tc in range(2):
                    c0 = tc * 512
                    pses = [pp.acquire() for _ in range(4)]
                    dps = [ps.begin_write() + hfb.wr + wf2h[0].wr
                           + wf2h[1].wr for ps in pses]
                    lasts = [None] * 4
                    for k in range(PC_HF):
                        wpc = wf2h[k // 16]
                        kk = k % 16
                        for j in range(4):
                            ps = pses[j]
                            tok = p.op("tensor", lambda e, kk=kk, j=j, c0=c0,
                                       k=k, ps=ps, wpc=wpc, hfb=hfb:
                                       e.matmul(
                                           ps.h[:],
                                           wpc.h[:, kk, j * 128:(j + 1) * 128],
                                           hfb.h[:, k, c0:c0 + 512],
                                           start=(k == 0),
                                           stop=(k == PC_HF - 1)),
                                       deps=dps[j])
                            hfb.read(tok)
                            wpc.read(tok)
                            lasts[j] = tok
                    for j in range(4):
                        do = dg * 4 + j
                        ps = pses[j]
                        ps.wrote(lasts[j])
                        tok = p.op("vector", lambda e, do=do, c0=c0, ps=ps:
                                   e.scalar_tensor_tensor(
                                       out=xc5.h[:, do, c0:c0 + 512],
                                       in0=ps.h[:],
                                       scalar=f2b_sb.h[:, do:do + 1],
                                       in1=xc5.h[:, do, c0:c0 + 512],
                                       op0=ALU.add, op1=ALU.add),
                                   deps=ps.wr + xc5.wr + f2b_sb.wr)
                        ps.read(tok)
                        xc5.wrote(tok)
                        ev.append(tok)
                wf2h[0].free()
                wf2h[1].free()
            hfb.free()
            st = dma_store(xspA[:, oc], xc5, deps=ev)
            dram_tok["xspA"][2 * oc] = [st]
            dram_tok["xspA"][2 * oc + 1] = [st]
        for b in (xc5, xn5):
            b.free()

    final_sts = []
    if "p2" in phases:
        final_sts += macaron_half("p2", xin, xspA,
                                  [dram_tok["xspA"][i] for i in range(4)])
    if "p3" in phases:
        phase_h2()
        final_sts = [t for c in dram_tok["h2sp"] + dram_tok["xasp"]
                     for t in c]
    lns_free()
    if "p4" in phases:
        mamba_dir("p4", 1, False)
        final_sts = [t for c in dram_tok["fwdsp"] for t in c]
    if "p5" in phases:
        mamba_dir("p5", 2, True)
        final_sts = [t for c in dram_tok["xspB"] for t in c]
    lns_alloc()
    if "p6" in phases:
        phase_ffn()
        final_sts = [t for c in dram_tok["xspA"] for t in c]
    if "p7" in phases:
        final_sts = macaron_half("p7", xspA, out, None, last=True)

    p.fence("sync", [final_sts])
    p.finalize_and_emit(nc)
    return nc


def _host_inputs(inputs):
    f = np.ascontiguousarray

    def vec(v, pc):
        return f(np.asarray(v).reshape(pc, 128).T)

    def vmat(v, pc, w):
        return f(np.asarray(v).reshape(pc, 128, w).transpose(1, 0, 2))

    g = {}
    for k, pre in ((1, "L0_attn_m1_"), (2, "L0_attn_m2_")):
        in_w = np.asarray(inputs[pre + "in_w"])
        g[f"inw{k}"] = f(in_w.reshape(16, 256, 8, 128).transpose(3, 0, 2, 1))
        out_w = np.asarray(inputs[pre + "out_w"])
        g[f"outw{k}"] = f(out_w.reshape(8, 128, 16, 128)
                          .transpose(3, 0, 2, 1))
        xp = np.asarray(inputs[pre + "xproj_w"])
        g[f"xpd{k}"] = f(xp[0:DTR].T.reshape(16, 128, DTR).transpose(1, 0, 2))
        g[f"xpb{k}"] = f(xp[DTR:DTR + DST].T.reshape(16, 128, DST)
                         .transpose(1, 0, 2))
        g[f"xpc{k}"] = f(xp[DTR + DST:].T.reshape(16, 128, DST)
                         .transpose(1, 0, 2))
        dt_w = np.asarray(inputs[pre + "dt_w"])
        g[f"dtw{k}"] = f(dt_w.reshape(16, 128, DTR).transpose(2, 0, 1))
        g[f"conv{k}"] = vmat(inputs[pre + "conv_w"], 16, 4)
        g[f"cb{k}"] = vec(inputs[pre + "conv_b"], 16)
        g[f"dtb{k}"] = vec(inputs[pre + "dt_b"], 16)
        g[f"alog{k}"] = vmat(inputs[pre + "Alog"], 16, DST)
        g[f"dp{k}"] = vec(inputs[pre + "D"], 16)
    g["ng"] = vec(inputs["L0_attn_norm_g"], 8)
    g["nb"] = vec(inputs["L0_attn_norm_b"], 8)
    g["fg"] = vec(inputs["L0_attn_ffn_g"], 8)
    g["fb"] = vec(inputs["L0_attn_ffn_b"], 8)
    ff1w = np.asarray(inputs["L0_attn_ff1_w"])
    g["ff1"] = f(ff1w.reshape(8, 512, 8, 128).transpose(3, 0, 2, 1))
    g["f1b"] = vec(inputs["L0_attn_ff1_b"], 32)
    ff2w = np.asarray(inputs["L0_attn_ff2_w"])
    g["ff2"] = f(ff2w.reshape(2, 512, 32, 128)
                 .transpose(3, 0, 2, 1))
    g["f2b"] = vec(inputs["L0_attn_ff2_b"], 8)
    wgm = np.asarray(inputs["L0_ffn_w_gate"])
    g["wg"] = f(wgm.reshape(12, 256, 8, 128).transpose(3, 0, 2, 1))
    wum = np.asarray(inputs["L0_ffn_w_up"])
    g["wu"] = f(wum.reshape(12, 256, 8, 128).transpose(3, 0, 2, 1))
    wdm = np.asarray(inputs["L0_ffn_w_down"])
    g["wd"] = f(wdm.T.reshape(24, 128, 1024).transpose(1, 0, 2))
    g["eyeF"] = np.eye(128, dtype=np.float32)
    g["jmatF"] = f(np.eye(128, dtype=np.float32)[::-1])
    sel = np.zeros((DST, DST, 128), dtype=np.float32)
    for s in range(DST):
        sel[s, s, :] = 1.0
    g["selF"] = sel
    return g


def _stage_x(xb):
    return np.ascontiguousarray(
        np.asarray(xb).reshape(2, 1024, 8, 128).transpose(3, 0, 2, 1))


def _unstage_out(o):
    return np.asarray(o).transpose(1, 3, 2, 0).reshape(2048, 1024)


_NC_CACHE = {}


def _get_nc():
    if "nc" not in _NC_CACHE:
        _NC_CACHE["nc"] = build()
    return _NC_CACHE["nc"]


def kernel(**inputs):
    nc = _get_nc()
    g = _host_inputs(inputs)
    x = np.asarray(inputs["L0_attn_x"])
    in_maps = []
    for b in range(NCORES):
        m = dict(g)
        m["xin"] = _stage_x(x[b])
        in_maps.append(m)
    res = run_bass_kernel_spmd(nc, in_maps, core_ids=list(range(NCORES)))
    outs = [_unstage_out(r["out"]) for r in res.results]
    return np.stack(outs, axis=0).astype(np.float32)


# version 13
# speedup vs baseline: 1.0197x; 1.0197x over previous
import numpy as np
import concourse.bass as bass
import concourse.mybir as mybir
from concourse.bass_utils import run_bass_kernel_spmd

F32 = mybir.dt.float32
BF16 = mybir.dt.bfloat16
I32 = mybir.dt.int32
AF = mybir.ActivationFunctionType
ALU = mybir.AluOpType

D = 1024
T = 2048
DI = 2048
DST = 16
DTR = 64
HQ = 3072
HF = 4096
EPS = 1e-5
QSC = 1.0 / 127.0
NCORES = 8

PC_D = D // 128
PC_DI = DI // 128
PC_HQ = HQ // 128
PC_HF = HF // 128

DT_SIZE = {F32: 4, BF16: 2, I32: 4}

ENGINES = ("sync", "scalar", "vector", "tensor", "gpsimd")


class Op:
    __slots__ = ("eng", "fn", "deps", "dma", "needed", "val", "pos")

    def __init__(self, eng, fn, deps, dma):
        self.eng = eng
        self.fn = fn
        self.deps = deps
        self.dma = dma
        self.needed = dma
        self.val = None
        self.pos = None


def _flat(deps):
    out = []
    stack = [deps]
    while stack:
        d = stack.pop()
        if d is None:
            continue
        if isinstance(d, Op):
            out.append(d)
        elif isinstance(d, (list, tuple)):
            stack.extend(d)
        else:
            raise TypeError(f"bad dep {d!r}")
    return out


class Prog:
    def __init__(self):
        self.ops = {e: [] for e in ENGINES}
        self.all_ops = []

    def op(self, eng, fn, deps=()):
        o = Op(eng, fn, _flat(deps), False)
        o.pos = len(self.ops[eng])
        self.ops[eng].append(o)
        self.all_ops.append(o)
        return o

    def dma(self, eng, fn, deps=()):
        o = Op(eng, fn, _flat(deps), True)
        o.pos = len(self.ops[eng])
        self.ops[eng].append(o)
        self.all_ops.append(o)
        return o

    def fence(self, eng, deps):
        return self.op(eng, None, deps)

    def finalize_and_emit(self, nc):
        for o in self.all_ops:
            for d in o.deps:
                assert d.fn is not None, "op depends on a fence (never incs)"
                if d.eng != o.eng or d.dma or o.dma:
                    d.needed = True
        cnt = {}
        for e in ENGINES:
            c = 0
            d = 0
            for o in self.ops[e]:
                if o.dma:
                    d += 1
                    o.val = 16 * d
                elif o.needed and o.fn is not None:
                    c += 1
                    o.val = c
            cnt[e] = (c, d)
        from contextlib import ExitStack
        with ExitStack() as st:
            sems = {}
            for e in ENGINES:
                c, d = cnt[e]
                if c > 0:
                    sems["c" + e] = st.enter_context(nc.semaphore("sem_c_" + e))
                if d > 0:
                    sems["d" + e] = st.enter_context(nc.semaphore("sem_d_" + e))
            block = st.enter_context(nc.Block())
            for e in ENGINES:
                if self.ops[e]:
                    self._emit_engine(block, e, sems)

    def _emit_engine(self, block, e, sems):
        ops = self.ops[e]

        def body(eng):
            hw = {}
            for o in ops:
                for dep in o.deps:
                    if dep.val is None:
                        continue
                    if dep.eng == e and not dep.dma and not o.dma \
                            and e != "vector":
                        continue
                    key = ("d" if dep.dma else "c") + dep.eng
                    if hw.get(key, 0) >= dep.val:
                        continue
                    eng.wait_ge(sems[key], dep.val)
                    hw[key] = dep.val
                if o.fn is None:
                    continue
                inst = o.fn(eng)
                if o.dma:
                    inst.then_inc(sems["d" + e], 16)
                elif o.needed:
                    inst.then_inc(sems["c" + e], 1)

        getattr(block, e)(body)


class SB:

    def __init__(self, nc):
        lo = (nc.sbuf_base + 63) & ~63
        hi = nc.sbuf_top
        self.nc = nc
        self.freelist = [(lo, hi, [])]
        self.ranges = {}
        self.n = 0

    def alloc(self, name, shape, dtype):
        bpp = int(np.prod(shape[1:])) * DT_SIZE[dtype]
        bpp = (bpp + 63) & ~63
        for i, (a, b, stale) in enumerate(self.freelist):
            if b - a >= bpp:
                self.freelist[i] = (a + bpp, b, stale)
                if self.freelist[i][0] == self.freelist[i][1]:
                    del self.freelist[i]
                self.n += 1
                h = self.nc.alloc_sbuf_tensor_at(
                    f"{name}_{self.n}", list(shape), dtype, offset=a)
                self.ranges[id(h)] = (a, a + bpp)
                return h, list(stale)
        raise MemoryError(f"SBUF alloc failed for {name} {shape} ({bpp}B/p); "
                          f"free={[(a, b) for a, b, _ in self.freelist]}")

    def free(self, h, tokens):
        a, b = self.ranges.pop(id(h))
        self.freelist.append((a, b, list(tokens)))
        self.freelist.sort(key=lambda s: s[0])
        merged = []
        for seg in self.freelist:
            if merged and merged[-1][1] == seg[0]:
                merged[-1] = (merged[-1][0], seg[1], merged[-1][2] + seg[2])
            else:
                merged.append((seg[0], seg[1], list(seg[2])))
        self.freelist = merged


class Buf:
    __slots__ = ("h", "wr", "rd", "sb")

    def __init__(self, h, stale, sb):
        self.h = h
        self.wr = list(stale)
        self.rd = []
        self.sb = sb

    def begin_write(self):
        d = self.rd + self.wr
        self.wr = []
        self.rd = []
        return d

    def wrote(self, tok):
        self.wr.append(tok)

    def read(self, tok):
        self.rd.append(tok)

    def free(self):
        self.sb.free(self.h, self.rd + self.wr)


def _psbuf(h):
    b = Buf.__new__(Buf)
    b.h = h
    b.wr = []
    b.rd = []
    b.sb = None
    return b


class PsumPool:
    def __init__(self, nc):
        self.banks = [_psbuf(nc.alloc_psum_tensor(f"pb{i}", [128, 512], F32))
                      for i in range(7)]
        self.bT = _psbuf(nc.alloc_psum_tensor("pbT", [128, 512], BF16))
        self.i = 0

    def acquire(self):
        b = self.banks[self.i % len(self.banks)]
        self.i += 1
        return b


def build(phases=("p2", "p3", "p4", "p5", "p6", "p7"), debug=False):
    phases = set(phases)
    nc = bass.Bass("TRN2", target_bir_lowering=False)
    p = Prog()
    sb = SB(nc)
    pp = PsumPool(nc)

    def din(name, shape, dtype=F32):
        return nc.dram_tensor(name, list(shape), dtype,
                              kind="ExternalInput").ap()

    def dspill(name, shape, dtype):
        kind = "ExternalOutput" if debug else "Internal"
        return nc.dram_tensor(name, list(shape), dtype, kind=kind).ap()

    xin = din("xin", [128, 2, 8, 1024])
    inw = {1: din("inw1", [128, 16, 8, 256]), 2: din("inw2", [128, 16, 8, 256])}
    outw = {1: din("outw1", [128, 8, 16, 128]), 2: din("outw2", [128, 8, 16, 128])}
    xpd = {1: din("xpd1", [128, 16, DTR]), 2: din("xpd2", [128, 16, DTR])}
    xpb = {1: din("xpb1", [128, 16, DST]), 2: din("xpb2", [128, 16, DST])}
    xpc = {1: din("xpc1", [128, 16, DST]), 2: din("xpc2", [128, 16, DST])}
    dtw = {1: din("dtw1", [DTR, 16, 128]), 2: din("dtw2", [DTR, 16, 128])}
    convw = {1: din("conv1", [128, 16, 4]), 2: din("conv2", [128, 16, 4])}
    cbv = {1: din("cb1", [128, 16]), 2: din("cb2", [128, 16])}
    dtbv = {1: din("dtb1", [128, 16]), 2: din("dtb2", [128, 16])}
    alogv = {1: din("alog1", [128, 16, DST]), 2: din("alog2", [128, 16, DST])}
    dpv = {1: din("dp1", [128, 16]), 2: din("dp2", [128, 16])}
    ng = din("ng", [128, 8]); nb = din("nb", [128, 8])
    fg = din("fg", [128, 8]); fb = din("fb", [128, 8])
    ff1 = din("ff1", [128, 8, 8, 512]); f1bi = din("f1b", [128, 32])
    ff2 = din("ff2", [128, 2, 32, 512]); f2bi = din("f2b", [128, 8])
    wg = din("wg", [128, 12, 8, 256], I32)
    wu = din("wu", [128, 12, 8, 256], I32)
    wd = din("wd", [128, 24, 1024], I32)
    eyeF = din("eyeF", [128, 128])
    jmatF = din("jmatF", [128, 128])
    selF = din("selF", [DST, DST, 128])

    out = nc.dram_tensor("out", [128, 2, 8, 1024], F32,
                         kind="ExternalOutput").ap()
    xspA = dspill("xspA", [128, 2, 8, 1024], F32)
    h2sp = dspill("h2sp", [128, 4, 8, 512], BF16)
    xasp = dspill("xasp", [128, 4, 8, 512], F32)
    fwdsp = dspill("fwdsp", [128, 4, 8, 512], BF16)
    xspB = dspill("xspB", [128, 4, 8, 512], F32)

    dram_tok = {"xspA": [[] for _ in range(4)], "h2sp": [[] for _ in range(4)],
                "xasp": [[] for _ in range(4)], "fwdsp": [[] for _ in range(4)],
                "xspB": [[] for _ in range(4)]}

    def mk(name, shape, dtype):
        h, stale = sb.alloc(name, shape, dtype)
        return Buf(h, stale, sb)

    def dma_load(buf, src, deps=(), cast=False, dst=None):
        eng = "gpsimd" if cast else "sync"
        dstap = buf.h[:] if dst is None else dst
        d = buf.begin_write() + _flat(deps)
        tok = p.dma(eng, lambda e, dstap=dstap, src=src: e.dma_start(
            out=dstap, in_=src), deps=d)
        buf.wrote(tok)
        return tok

    def dma_load_into(buf, dst, src, deps=(), cast=False):
        eng = "gpsimd" if cast else "sync"
        d = list(buf.wr) + list(buf.rd) + _flat(deps)
        tok = p.dma(eng, lambda e, dst=dst, src=src: e.dma_start(
            out=dst, in_=src), deps=d)
        buf.wrote(tok)
        return tok

    def dma_store(dst, buf, deps=(), src=None, eng="sync"):
        srcap = buf.h[:] if src is None else src
        tok = p.dma(eng, lambda e, dst=dst, srcap=srcap: e.dma_start(
            out=dst, in_=srcap), deps=list(buf.wr) + _flat(deps))
        buf.read(tok)
        return tok

    def resident(name, shape, dtype, src, cast=False):
        b = mk(name, shape, dtype)
        dma_load(b, src, cast=cast)
        return b

    conv_sb = {k: resident(f"conv{k}", [128, 16, 4], F32, convw[k])
               for k in (1, 2)}
    cb_sb = {k: resident(f"cb{k}", [128, 16], F32, cbv[k]) for k in (1, 2)}
    dtb_sb = {k: resident(f"dtb{k}", [128, 16], F32, dtbv[k]) for k in (1, 2)}
    dp_sb = {k: resident(f"dp{k}", [128, 16], F32, dpv[k]) for k in (1, 2)}
    ng_sb = resident("ng", [128, 8], F32, ng)
    nb_sb = resident("nb", [128, 8], F32, nb)
    fg_sb = resident("fg", [128, 8], F32, fg)
    fb_sb = resident("fb", [128, 8], F32, fb)
    f1b_sb = resident("f1b", [128, 32], F32, f1bi)
    f2b_sb = resident("f2b", [128, 8], F32, f2bi)
    alog_sb = {k: resident(f"alog{k}", [128, 16, DST], F32, alogv[k])
               for k in (1, 2)}
    eye_sb = resident("eye", [128, 128], BF16, eyeF, cast=True)
    jmat_sb = resident("jmat", [128, 128], BF16, jmatF, cast=True)
    sel_sb = resident("sel", [DST, DST, 128], BF16, selF, cast=True)
    xpd_sb = {k: resident(f"xpd{k}", [128, 16, DTR], BF16, xpd[k], cast=True)
              for k in (1, 2)}
    xpb_sb = {k: resident(f"xpb{k}", [128, 16, DST], BF16, xpb[k], cast=True)
              for k in (1, 2)}
    xpc_sb = {k: resident(f"xpc{k}", [128, 16, DST], BF16, xpc[k], cast=True)
              for k in (1, 2)}
    dtw_sb = {k: resident(f"dtw{k}", [DTR, 16, 128], BF16, dtw[k], cast=True)
              for k in (1, 2)}

    ones_col_b = mk("ones_col_b", [128, 1], BF16)
    ones_col_b.wrote(p.op("vector", lambda e: e.memset(ones_col_b.h[:], 1.0)))
    ones_row_b = mk("ones_row_b", [1, 128], BF16)
    ones_row_b.wrote(p.op("vector", lambda e: e.memset(ones_row_b.h[:], 1.0)))
    eps_c = mk("eps_c", [1, 1], F32)
    eps_c.wrote(p.op("vector", lambda e: e.memset(eps_c.h[:], EPS)))
    one_c = mk("one_c", [128, 1], F32)
    one_c.wrote(p.op("vector", lambda e: e.memset(one_c.h[:], 1.0)))
    qsc_c = mk("qsc_c", [128, 1], F32)
    qsc_c.wrote(p.op("vector", lambda e: e.memset(qsc_c.h[:], QSC)))

    A_sb = {}
    for k in (1, 2):
        a = mk(f"A{k}", [128, 16, DST], F32)
        src = alog_sb[k]
        d = a.begin_write()
        t1 = p.op("scalar", lambda e, a=a, src=src: e.activation(
            out=a.h[:], in_=src.h[:], func=AF.Exp), deps=d + src.wr)
        src.read(t1)
        t2 = p.op("vector", lambda e, a=a: e.tensor_scalar_mul(
            a.h[:], a.h[:], -1.0), deps=[t1])
        a.wrote(t2)
        src.free()
        A_sb[k] = a

    lns = {}

    def lns_alloc():
        lns["scr_rows"] = [mk(f"scr_rows{i}", [1, 3, 512], F32)
                           for i in range(2)]
        lns["scr_rows_b"] = [mk(f"scr_rows_b{i}", [1, 2, 512], BF16)
                             for i in range(2)]
        lns["sq_t"] = mk("sq_t", [128, 2, 512], BF16)
        lns["xb_t"] = mk("xb_t", [128, 2, 512], BF16)
        lns["lnt"] = mk("lnt", [128, 2, 512], BF16)
        lns["par"] = 0

    def lns_free():
        for k in list(lns):
            v = lns.pop(k)
            if isinstance(v, list):
                for b in v:
                    b.free()
            elif k != "par":
                v.free()

    lns_alloc()

    def ln_512(xc, xc_view, src_bf16=False):
        lpar = lns["par"]
        lns["par"] = 1 - lpar
        scr_rows = lns["scr_rows"][lpar]
        scr_rows_b = lns["scr_rows_b"][lpar]
        sq_t = lns["sq_t"]
        xb_t = lns["xb_t"]
        psS = pp.acquire()
        psQ = pp.acquire()
        dS = psS.begin_write()
        dQ = psQ.begin_write()
        mm_s = mm_q = None
        for i in range(PC_D):
            par = i % 2
            sqv = sq_t.h[:, par, :]
            xbv = xb_t.h[:, par, :]
            if src_bf16:
                xv_b = xc_view(i)
                tcast = None
            else:
                dxb = xb_t.begin_write() if par == 0 and i < 2 else list(xb_t.wr) + list(xb_t.rd)
                tcast = p.op("scalar", lambda e, i=i, xbv=xbv: e.activation(
                    out=xbv, in_=xc_view(i), func=AF.Copy),
                    deps=dxb + xc.wr)
                xc.read(tcast)
                xb_t.wrote(tcast)
                xv_b = xbv
            dsq = sq_t.begin_write() if i < 2 else list(sq_t.wr) + list(sq_t.rd)
            tsq = p.op("scalar", lambda e, i=i, sqv=sqv: e.activation(
                out=sqv, in_=xc_view(i), func=AF.Square),
                deps=dsq + xc.wr)
            xc.read(tsq)
            sq_t.wrote(tsq)
            mm_s = p.op("tensor", lambda e, i=i, xv_b=xv_b: e.matmul(
                psS.h[0:1, :], ones_col_b.h[:], xv_b,
                start=(i == 0), stop=(i == PC_D - 1)),
                deps=dS + ([tcast] if tcast else xc.wr) + ones_col_b.wr)
            if tcast is None:
                xc.read(mm_s)
            else:
                xb_t.read(mm_s)
            mm_q = p.op("tensor", lambda e, i=i, sqv=sqv: e.matmul(
                psQ.h[0:1, :], ones_col_b.h[:], sqv,
                start=(i == 0), stop=(i == PC_D - 1)),
                deps=dQ + [tsq] + ones_col_b.wr)
            sq_t.read(mm_q)
        psS.wrote(mm_s)
        psQ.wrote(mm_q)
        mu = scr_rows.h[:, 0, :]
        q = scr_rows.h[:, 1, :]
        m2 = scr_rows.h[:, 2, :]
        rd = scr_rows.begin_write()
        t1 = p.op("vector", lambda e, mu=mu: e.tensor_scalar_mul(
            mu, psS.h[0:1, :], 1.0 / D), deps=rd + psS.wr)
        psS.read(t1)
        t2 = p.op("vector", lambda e, q=q: e.tensor_scalar_mul(
            q, psQ.h[0:1, :], 1.0 / D), deps=rd + psQ.wr)
        psQ.read(t2)
        t3 = p.op("vector", lambda e, mu=mu, m2=m2: e.tensor_tensor(
            out=m2, in0=mu, in1=mu, op=ALU.mult), deps=[t1])
        t4 = p.op("vector", lambda e, q=q, m2=m2: e.tensor_tensor(
            out=q, in0=q, in1=m2, op=ALU.subtract), deps=[t2, t3])
        t5 = p.op("scalar", lambda e, q=q, m2=m2: e.activation(
            out=m2, in_=q, func=AF.Ln, bias=eps_c.h[:]),
            deps=[t4] + eps_c.wr)
        t6 = p.op("scalar", lambda e, q=q, m2=m2: e.activation(
            out=q, in_=m2, func=AF.Exp, scale=-0.5), deps=[t5])
        scr_rows.wr = [t1, t3, t4, t5, t6]
        rb = scr_rows_b.begin_write()
        t7 = p.op("vector", lambda e, q=q: e.tensor_copy(
            out=scr_rows_b.h[:, 0, :], in_=q), deps=[t6] + rb)
        t8 = p.op("vector", lambda e, mu=mu: e.tensor_copy(
            out=scr_rows_b.h[:, 1, :], in_=mu), deps=[t1, t5] + rb)
        scr_rows.read(t7)
        scr_rows.read(t8)
        scr_rows_b.wr = [t7, t8]
        psR = pp.acquire()
        psM = pp.acquire()
        dR = psR.begin_write()
        dM = psM.begin_write()
        tR = p.op("tensor", lambda e: e.matmul(
            psR.h[:], ones_row_b.h[:], scr_rows_b.h[0:1, 0, :],
            start=True, stop=True), deps=dR + [t7] + ones_row_b.wr)
        scr_rows_b.read(tR)
        psR.wrote(tR)
        tM = p.op("tensor", lambda e: e.matmul(
            psM.h[:], ones_row_b.h[:], scr_rows_b.h[0:1, 1, :],
            start=True, stop=True), deps=dM + [t8] + ones_row_b.wr)
        scr_rows_b.read(tM)
        psM.wrote(tM)
        return psM, psR

    def ln_apply(xc, xc_view, psM, psR, out_write):
        lnt = lns["lnt"]
        toks = []
        for i in range(PC_D):
            par = i % 2
            lv = lnt.h[:, par, :]
            dl = lnt.begin_write() if i < 2 else list(lnt.wr) + list(lnt.rd)
            tok1 = p.op("vector", lambda e, i=i, lv=lv: e.tensor_tensor(
                out=lv, in0=xc_view(i), in1=psM.h[:], op=ALU.subtract),
                deps=dl + xc.wr + psM.wr)
            xc.read(tok1)
            psM.read(tok1)
            lnt.wrote(tok1)
            tok2 = out_write(i, lv, [tok1] + psR.wr)
            psR.read(tok2)
            lnt.read(tok2)
            toks.append(tok2)
        return toks

    def macaron_half(tag, src_dram, dst_dram, dst_tok_reg, last=False):
        xsc = mk(tag + "_xsc", [128, 8, 2048], BF16)
        xcm = mk(tag + "_xcm", [128, 8, 1024], F32)
        bw_xsc = xsc.begin_write()
        xsc_toks = []
        for oc in range(2):
            dma_load(xcm, src_dram[:, oc], deps=[dram_tok["xspA"][2 * oc],
                                                 dram_tok["xspA"][2 * oc + 1]]
                     if src_dram is xspA else ())
            for sc in range(2):
                s0 = sc * 512
                g0 = oc * 1024 + s0

                def xv(i, s0=s0):
                    return xcm.h[:, i, s0:s0 + 512]

                psM, psR = ln_512(xcm, xv)

                def ow(i, lv, deps, g0=g0, psR=psR):
                    tok = p.op("vector", lambda e, i=i, g0=g0, psR=psR,
                               lv=lv: e.tensor_tensor(
                                   out=xsc.h[:, i, g0:g0 + 512],
                                   in0=lv, in1=psR.h[:], op=ALU.mult),
                               deps=deps + bw_xsc)
                    xsc.wrote(tok)
                    return tok

                xsc_toks += ln_apply(xcm, xv, psM, psR, ow)
        xcm.free()
        if last:
            lns_free()

        hbuf = [mk(tag + f"_hbuf{i}", [128, 8, 2048], BF16)
                for i in range(3)]
        xcm2 = mk(tag + "_xcm2", [128, 8, 512], F32)
        wg_l = [mk(tag + f"_wgl{i}", [128, 8, 256], BF16) for i in range(2)]
        wu_l = [mk(tag + f"_wul{i}", [128, 8, 256], BF16) for i in range(2)]
        sg_t = mk(tag + "_sg", [128, 512], BF16)
        bw_h = hbuf[0].begin_write() + hbuf[1].begin_write() \
            + hbuf[2].begin_write()
        for sl in range(12):
            wgb = wg_l[sl % 2]
            wub = wu_l[sl % 2]
            dma_load(wgb, wg[:, sl], cast=True)
            dma_load(wub, wu[:, sl], cast=True)
            for mb in range(2):
                m = sl * 2 + mb
                for tc in range(4):
                    c0 = tc * 512
                    psG = pp.acquire()
                    psU = pp.acquire()
                    dG = psG.begin_write() + wgb.wr + xsc.wr
                    dU = psU.begin_write() + wub.wr + xsc.wr
                    tg = tu = None
                    for kc in range(PC_D):
                        tg = p.op("tensor", lambda e, kc=kc, mb=mb, c0=c0,
                                  wgb=wgb, psG=psG: e.matmul(
                                      psG.h[:],
                                      wgb.h[:, kc, mb * 128:(mb + 1) * 128],
                                      xsc.h[:, kc, c0:c0 + 512],
                                      start=(kc == 0), stop=(kc == PC_D - 1)),
                                  deps=dG)
                        wgb.read(tg)
                        xsc.read(tg)
                        tu = p.op("tensor", lambda e, kc=kc, mb=mb, c0=c0,
                                  wub=wub, psU=psU: e.matmul(
                                      psU.h[:],
                                      wub.h[:, kc, mb * 128:(mb + 1) * 128],
                                      xsc.h[:, kc, c0:c0 + 512],
                                      start=(kc == 0), stop=(kc == PC_D - 1)),
                                  deps=dU)
                        wub.read(tu)
                        xsc.read(tu)
                    psG.wrote(tg)
                    psU.wrote(tu)
                    dsg = sg_t.begin_write()
                    ts_ = p.op("scalar", lambda e, psG=psG: e.activation(
                        out=sg_t.h[:], in_=psG.h[:], func=AF.Silu,
                        scale=qsc_c.h[:]), deps=dsg + psG.wr + qsc_c.wr)
                    psG.read(ts_)
                    sg_t.wrote(ts_)
                    hpc = hbuf[m // 8]
                    th = p.op("vector", lambda e, m=m, c0=c0, psU=psU,
                              hpc=hpc: e.tensor_tensor(
                                  out=hpc.h[:, m % 8, c0:c0 + 512],
                                  in0=sg_t.h[:], in1=psU.h[:], op=ALU.mult),
                              deps=[ts_] + psU.wr + bw_h)
                    psU.read(th)
                    sg_t.read(th)
                    hpc.wrote(th)
        xsc.free()
        for b in wg_l + wu_l:
            b.free()
        sg_t.free()

        wdres = [mk(tag + f"_wdres{i}", [128, 4, 1024], BF16)
                 for i in range(6)]
        for i, wpc in enumerate(wdres):
            dma_load(wpc, wd[:, 4 * i:4 * i + 4, :], cast=True)
        stores = []
        for oc in range(2):
            for tc2 in range(2):
                tb = oc * 2 + tc2
                c0 = tb * 512
                dma_load(xcm2, src_dram[:, oc, :, tc2 * 512:tc2 * 512 + 512],
                         deps=[dram_tok["xspA"][tb]]
                         if src_dram is xspA else ())
                ev = []
                for dg in range(2):
                    pses = [pp.acquire() for _ in range(4)]
                    wdtoks = [t for w in wdres for t in w.wr]
                    dps = [ps.begin_write() + hbuf[0].wr + hbuf[1].wr
                           + hbuf[2].wr + wdtoks for ps in pses]
                    lasts = [None] * 4
                    for k in range(24):
                        wpc = wdres[k // 4]
                        kk = k % 4
                        hpc = hbuf[k // 8]
                        for j in range(4):
                            do = dg * 4 + j
                            ps = pses[j]
                            tok = p.op("tensor", lambda e, kk=kk, do=do,
                                       c0=c0, k=k, ps=ps, wpc=wpc, hpc=hpc:
                                       e.matmul(
                                           ps.h[:],
                                           wpc.h[:, kk, do * 128:(do + 1) * 128],
                                           hpc.h[:, k % 8, c0:c0 + 512],
                                           start=(k == 0), stop=(k == 23)),
                                       deps=dps[j])
                            hpc.read(tok)
                            wpc.read(tok)
                            lasts[j] = tok
                    for j in range(4):
                        do = dg * 4 + j
                        ps = pses[j]
                        ps.wrote(lasts[j])
                        tok = p.op("vector", lambda e, do=do, ps=ps:
                                   e.scalar_tensor_tensor(
                                       out=xcm2.h[:, do, :], in0=ps.h[:],
                                       scalar=0.5 * QSC * QSC,
                                       in1=xcm2.h[:, do, :],
                                       op0=ALU.mult, op1=ALU.add),
                                   deps=ps.wr + xcm2.wr)
                        ps.read(tok)
                        xcm2.wrote(tok)
                        ev.append(tok)
                st = dma_store(dst_dram[:, oc, :, tc2 * 512:tc2 * 512 + 512],
                               xcm2, deps=ev)
                if dst_tok_reg is not None:
                    dst_tok_reg[tb].append(st)
                stores.append(st)
        for b in hbuf + wdres + [xcm2]:
            b.free()
        return stores

    def phase_h2():
        xc = mk("p3_xc", [128, 8, 512], F32)
        hl = mk("p3_hl", [128, 8, 512], F32)
        h2c = mk("p3_h2c", [128, 8, 512], BF16)
        lnt2 = mk("p3_lnt2", [128, 512], F32)
        for ci in range(4):
            oc, tc2 = divmod(ci, 2)
            dma_load(xc, xspA[:, oc, :, tc2 * 512:tc2 * 512 + 512],
                     deps=[dram_tok["xspA"][ci]])

            def xv(i):
                return xc.h[:, i, :]

            psM, psR = ln_512(xc, xv)
            bw_hl = hl.begin_write()

            def ow(i, lv, deps, psR=psR):
                tok = p.op("vector", lambda e, i=i, psR=psR, lv=lv:
                           e.tensor_tensor(
                    out=hl.h[:, i, :], in0=lv, in1=psR.h[:],
                    op=ALU.mult), deps=deps + bw_hl)
                hl.wrote(tok)
                return tok

            ln_apply(xc, xv, psM, psR, ow)

            def xv2(i):
                return hl.h[:, i, :]

            psM2, psR2 = ln_512(hl, xv2)
            bw_h2 = h2c.begin_write()

            def ow2(i, lv, deps, psR2=psR2):
                d2 = lnt2.begin_write()
                tok2 = p.op("vector", lambda e, i=i, psR2=psR2, lv=lv:
                            e.tensor_tensor(out=lnt2.h[:], in0=lv,
                                            in1=psR2.h[:], op=ALU.mult),
                            deps=deps + d2)
                lnt2.wrote(tok2)
                tok3 = p.op("vector", lambda e, i=i: e.tensor_scalar(
                    out=h2c.h[:, i, :], in0=lnt2.h[:],
                    scalar1=ng_sb.h[:, i:i + 1], scalar2=nb_sb.h[:, i:i + 1],
                    op0=ALU.mult, op1=ALU.add),
                    deps=[tok2] + bw_h2 + ng_sb.wr + nb_sb.wr)
                lnt2.read(tok3)
                h2c.wrote(tok3)
                return tok3

            ln_apply(hl, xv2, psM2, psR2, ow2)
            dram_tok["h2sp"][ci].append(dma_store(h2sp[:, ci], h2c))
            dram_tok["xasp"][ci].append(dma_store(xasp[:, ci], hl))
        for b in (xc, hl, h2c, lnt2):
            b.free()

    def mamba_dir(tag, widx, rev):
        TCS = 512
        if widx == 2:
            for d_ in (conv_sb, cb_sb, dtb_sb, dp_sb, A_sb, xpd_sb,
                       xpb_sb, xpc_sb, dtw_sb):
                if 1 in d_:
                    d_.pop(1).free()
        xsraw = mk(tag + "_xsraw", [128, 16, TCS + 3], BF16)
        xs_c = mk(tag + "_xs", [128, 16, TCS], BF16)
        sz_p = [mk(tag + f"_sz{i}", [128, 16, TCS], BF16) for i in range(2)]
        dtx = mk(tag + "_dtx", [128, 16, TCS], BF16)
        hcar = mk(tag + "_hcar", [128, 32, 8, 1], F32)
        zc = mk(tag + "_zc", [128, 1], F32)
        f_t = mk(tag + "_ft", [128, 8, 1], BF16)
        dtr16 = mk(tag + "_dtr", [DTR, TCS], BF16)
        Brow = mk(tag + "_Brow", [DST, TCS], BF16)
        Crow = mk(tag + "_Crow", [DST, TCS], BF16)
        ysum = mk(tag + "_ys", [128, TCS], BF16)
        ytmp = mk(tag + "_yt", [128, TCS], BF16)
        spt = [mk(tag + f"_sp{i}", [128, TCS], BF16) for i in range(2)]
        rtmp = mk(tag + "_rt", [128, 4, 128], BF16)
        ctmp = mk(tag + "_ct", [128, TCS], BF16)
        dt_t = mk(tag + "_dt", [128, 16, TCS], BF16)
        Bball = mk(tag + "_Bb", [128, DST, TCS], BF16)
        Cball = mk(tag + "_Cb", [128, DST, TCS], BF16)
        dA3p = [mk(tag + f"_dA{i}", [128, 8, TCS], BF16) for i in range(2)]
        dB3 = mk(tag + "_dB", [128, 8, TCS], BF16)
        wil = [mk(tag + f"_wil{i}", [128, 8, 256], BF16)
               for i in range(2)]
        wol = mk(tag + "_wol", [128, 16, 128], BF16)
        st_f = [mk(tag + f"_stf{i}", [128, TCS], BF16) for i in range(2)]
        if rev:
            st_x = [mk(tag + "_stx0", [128, TCS], BF16)]
            st_m = [mk(tag + "_stm0", [128, TCS], BF16)]
            st_g = [mk(tag + "_stg0", [128, TCS], BF16)]
        else:
            st_x = st_m = st_g = []

        zt = p.op("vector", lambda e: e.memset(hcar.h[:], 0.0),
                  deps=hcar.begin_write())
        hcar.wrote(zt)
        ztz = p.op("vector", lambda e: e.memset(zc.h[:], 0.0),
                   deps=zc.begin_write())
        zc.wrote(ztz)
        zt2 = p.op("vector", lambda e: e.memset(xsraw.h[:, :, 0:3], 0.0),
                   deps=xsraw.begin_write())
        xsraw.wrote(zt2)
        state = {"tail": [zt2], "unit": 0}

        conv_w = conv_sb[widx]
        cb_w = cb_sb[widx]
        dtb_w = dtb_sb[widx]
        dp_w = dp_sb[widx]
        A_w = A_sb[widx]
        xpd_w = xpd_sb[widx]
        xpb_w = xpb_sb[widx]
        xpc_w = xpc_sb[widx]
        dtw_w = dtw_sb[widx]

        def prep_a(gc):
            raise RuntimeError("unused")

        def prep_gen(gc):
            sz_c = sz_p[gc % 2]
            hcm = mk(tag + f"_hcm{gc}", [128, 8, TCS], BF16)
            if not rev:
                dma_load(hcm, h2sp[:, gc], deps=[dram_tok["h2sp"][gc]])
            else:
                ci = 3 - gc
                bw_hcm = hcm.begin_write()
                for j in range(4):
                    src0 = 512 - (j + 1) * 128
                    tok = p.dma("sync", lambda e, j=j, src0=src0, hcm=hcm:
                                e.dma_start(
                                    out=hcm.h[:, :, j * 128:(j + 1) * 128],
                                    in_=h2sp[:, ci, :, src0:src0 + 128]),
                                deps=bw_hcm + _flat([dram_tok["h2sp"][ci]]))
                    hcm.wrote(tok)
                slot_tok = [{"wr": [], "rd": []} for _ in range(4)]
                rslot_tok = [{"wr": [], "rd": []} for _ in range(4)]
                bT = pp.bT
                for ti in range(32):
                    j, pc = divmod(ti, PC_D)
                    sl = ti % 4
                    stk = slot_tok[sl]
                    rtk = rslot_tok[sl]
                    dT = stk["wr"] + stk["rd"] + hcm.wr + eye_sb.wr
                    tk1 = p.op("tensor", lambda e, pc=pc, j=j, sl=sl,
                               hcm=hcm: e.transpose(
                                   bT.h[:, sl * 128:(sl + 1) * 128],
                                   hcm.h[:, pc, j * 128:(j + 1) * 128],
                                   eye_sb.h[:]), deps=dT)
                    hcm.read(tk1)
                    eye_sb.read(tk1)
                    stk["wr"] = [tk1]
                    stk["rd"] = []
                    tk2 = p.op("scalar", lambda e, sl=sl: e.activation(
                        out=rtmp.h[:, sl, :],
                        in_=bT.h[:, sl * 128:(sl + 1) * 128],
                        func=AF.Copy),
                        deps=[tk1] + rtk["wr"] + rtk["rd"])
                    stk["rd"] = [tk2]
                    rtk["wr"] = [tk2]
                    rtk["rd"] = []
                    psF = pp.acquire()
                    dF = psF.begin_write() + [tk2] + jmat_sb.wr
                    tk3 = p.op("tensor", lambda e, sl=sl, psF=psF: e.matmul(
                        psF.h[:, 0:128], rtmp.h[:, sl, :], jmat_sb.h[:],
                        start=True, stop=True), deps=dF)
                    rtk["rd"] = [tk3]
                    jmat_sb.read(tk3)
                    psF.wrote(tk3)
                    tk4 = p.op("scalar", lambda e, pc=pc, j=j, psF=psF,
                               hcm=hcm: e.activation(
                                   out=hcm.h[:, pc, j * 128:(j + 1) * 128],
                                   in_=psF.h[:, 0:128], func=AF.Copy),
                               deps=[tk3, tk1] + hcm.wr)
                    psF.read(tk4)
                    hcm.wrote(tk4)

            yield
            bw_xsm = xsraw.rd + xsraw.wr
            xsraw.rd = []
            xsraw.wr = list(state["tail"])
            bw_sz = sz_c.begin_write()
            state["xsraw_tok"] = [None] * PC_DI
            for sl in range(16):
                wb = wil[sl % 2]
                dma_load(wb, inw[widx][:, sl], cast=True)
                for mb in range(2):
                    m = sl * 2 + mb
                    ps = pp.acquire()
                    dmm = ps.begin_write() + wb.wr + hcm.wr
                    tok = None
                    for kc in range(PC_D):
                        tok = p.op("tensor", lambda e, kc=kc, mb=mb,
                                   ps=ps, hcm=hcm, wb=wb: e.matmul(
                                       ps.h[:],
                                       wb.h[:, kc, mb * 128:(mb + 1) * 128],
                                       hcm.h[:, kc, :],
                                       start=(kc == 0),
                                       stop=(kc == PC_D - 1)), deps=dmm)
                        wb.read(tok)
                        hcm.read(tok)
                    ps.wrote(tok)
                    if m < PC_DI:
                        tok2 = p.op("scalar", lambda e, m=m, ps=ps:
                                    e.activation(
                                        out=xsraw.h[:, m, 3:3 + TCS],
                                        in_=ps.h[:], func=AF.Copy),
                                    deps=ps.wr + bw_xsm)
                        xsraw.wrote(tok2)
                        state["xsraw_tok"][m] = tok2
                    else:
                        tok2 = p.op("scalar", lambda e, m=m, ps=ps,
                                    sz_c=sz_c: e.activation(
                                        out=sz_c.h[:, m - PC_DI, :],
                                        in_=ps.h[:], func=AF.Silu),
                                    deps=ps.wr + bw_sz)
                        sz_c.wrote(tok2)
                    ps.read(tok2)
                yield
            hcm.free()

            psD = pp.acquire(); psB_ = pp.acquire(); psC_ = pp.acquire()
            dD = psD.begin_write() + xpd_w.wr
            dB = psB_.begin_write() + xpb_w.wr
            dC = psC_.begin_write() + xpc_w.wr
            tD = tB = tC = None
            for dc in range(PC_DI):
                d0 = ctmp.begin_write() + [state["xsraw_tok"][dc]] \
                    + state["tail"] + conv_w.wr
                tk = p.op("vector", lambda e, dc=dc: e.tensor_scalar_mul(
                    ctmp.h[:], xsraw.h[:, dc, 0:TCS],
                    conv_w.h[:, dc, 0:1]), deps=d0)
                xsraw.read(tk)
                conv_w.read(tk)
                ctmp.wrote(tk)
                for kk in range(1, 4):
                    tk = p.op("vector", lambda e, dc=dc, kk=kk:
                              e.scalar_tensor_tensor(
                                  out=ctmp.h[:],
                                  in0=xsraw.h[:, dc, kk:kk + TCS],
                                  scalar=conv_w.h[:, dc, kk:kk + 1],
                                  in1=ctmp.h[:], op0=ALU.mult, op1=ALU.add),
                              deps=[tk])
                    xsraw.read(tk)
                    ctmp.wr = [tk]
                tk2 = p.op("scalar", lambda e, dc=dc: e.activation(
                    out=xs_c.h[:, dc, :], in_=ctmp.h[:], func=AF.Silu,
                    bias=cb_w.h[:, dc:dc + 1]),
                    deps=[tk] + list(xs_c.rd) + list(xs_c.wr) + cb_w.wr)
                ctmp.read(tk2)
                xs_c.wrote(tk2)
                kc = dc
                tD = p.op("tensor", lambda e, kc=kc, psD=psD: e.matmul(
                    psD.h[0:DTR, :], xpd_w.h[:, kc, :], xs_c.h[:, kc, :],
                    start=(kc == 0), stop=(kc == PC_DI - 1)),
                    deps=dD + [tk2])
                xs_c.read(tD)
                xpd_w.read(tD)
                tB = p.op("tensor", lambda e, kc=kc, psB_=psB_: e.matmul(
                    psB_.h[0:DST, :], xpb_w.h[:, kc, :], xs_c.h[:, kc, :],
                    start=(kc == 0), stop=(kc == PC_DI - 1)),
                    deps=dB + [tk2])
                xs_c.read(tB)
                xpb_w.read(tB)
                tC = p.op("tensor", lambda e, kc=kc, psC_=psC_: e.matmul(
                    psC_.h[0:DST, :], xpc_w.h[:, kc, :], xs_c.h[:, kc, :],
                    start=(kc == 0), stop=(kc == PC_DI - 1)),
                    deps=dC + [tk2])
                xs_c.read(tC)
                xpc_w.read(tC)
                yield
            ttail = p.op("vector", lambda e: e.tensor_copy(
                out=xsraw.h[:, :, 0:3], in_=xsraw.h[:, :, TCS:TCS + 3]),
                deps=xsraw.rd + xsraw.wr)
            state["tail"] = [ttail]
            xsraw.wrote(ttail)

            psD.wrote(tD); psB_.wrote(tB); psC_.wrote(tC)
            e1 = p.op("scalar", lambda e, psD=psD: e.activation(
                out=dtr16.h[:], in_=psD.h[0:DTR, :], func=AF.Copy),
                deps=psD.wr + dtr16.begin_write())
            psD.read(e1); dtr16.wrote(e1)
            e2 = p.op("scalar", lambda e, psB_=psB_: e.activation(
                out=Brow.h[:], in_=psB_.h[0:DST, :], func=AF.Copy),
                deps=psB_.wr + Brow.begin_write())
            psB_.read(e2); Brow.wrote(e2)
            e3 = p.op("scalar", lambda e, psC_=psC_: e.activation(
                out=Crow.h[:], in_=psC_.h[0:DST, :], func=AF.Copy),
                deps=psC_.wr + Crow.begin_write())
            psC_.read(e3); Crow.wrote(e3)

            bw_B = Bball.begin_write()
            bw_C = Cball.begin_write()
            for s in range(DST):
                for (row, ball, bw) in ((Brow, Bball, bw_B),
                                        (Crow, Cball, bw_C)):
                    psbc = pp.acquire()
                    dbc = psbc.begin_write() + row.wr + sel_sb.wr
                    tk = p.op("tensor", lambda e, s=s, row=row, psbc=psbc:
                              e.matmul(psbc.h[:], sel_sb.h[:, s, :],
                                       row.h[:], start=True, stop=True),
                              deps=dbc)
                    row.read(tk)
                    sel_sb.read(tk)
                    psbc.wrote(tk)
                    tk2 = p.op("scalar", lambda e, s=s, ball=ball,
                               psbc=psbc: e.activation(
                                   out=ball.h[:, s, :], in_=psbc.h[:],
                                   func=AF.Copy), deps=psbc.wr + bw)
                    psbc.read(tk2)
                    ball.wrote(tk2)

        def prep_b(gc):
            bw_dt = dt_t.begin_write()
            state["dt_tok"] = [None] * PC_DI
            state["dtx_tok"] = [None] * PC_DI
            for dc in range(PC_DI):
                psdt = pp.acquire()
                ddt = psdt.begin_write() + dtr16.wr + dtw_w.wr
                tk = p.op("tensor", lambda e, dc=dc, psdt=psdt: e.matmul(
                    psdt.h[:], dtw_w.h[:, dc, :], dtr16.h[:],
                    start=True, stop=True), deps=ddt)
                dtr16.read(tk)
                dtw_w.read(tk)
                psdt.wrote(tk)
                sp1, sp2 = spt
                a1 = p.op("scalar", lambda e, dc=dc, psdt=psdt, sp1=sp1:
                          e.activation(out=sp1.h[:], in_=psdt.h[:],
                                       func=AF.Abs,
                                       bias=dtb_w.h[:, dc:dc + 1]),
                          deps=psdt.wr + sp1.begin_write() + dtb_w.wr)
                psdt.read(a1); sp1.wrote(a1)
                a2 = p.op("scalar", lambda e, sp1=sp1, sp2=sp2:
                          e.activation(out=sp2.h[:], in_=sp1.h[:],
                                       func=AF.Exp, scale=-1.0),
                          deps=[a1] + sp2.begin_write())
                sp1.read(a2); sp2.wrote(a2)
                a3 = p.op("scalar", lambda e, sp1=sp1, sp2=sp2:
                          e.activation(out=sp1.h[:], in_=sp2.h[:],
                                       func=AF.Ln, bias=one_c.h[:]),
                          deps=[a2, a1] + one_c.wr)
                sp2.read(a3); sp1.wr = [a3]; sp1.rd = []
                a4 = p.op("scalar", lambda e, dc=dc, psdt=psdt, sp2=sp2:
                          e.activation(out=sp2.h[:], in_=psdt.h[:],
                                       func=AF.Relu,
                                       bias=dtb_w.h[:, dc:dc + 1]),
                          deps=psdt.wr + [a3] + dtb_w.wr)
                psdt.read(a4); sp2.wr = [a4]; sp2.rd = []
                a5 = p.op("vector", lambda e, dc=dc, sp1=sp1, sp2=sp2:
                          e.tensor_tensor(out=dt_t.h[:, dc, :],
                                          in0=sp1.h[:], in1=sp2.h[:],
                                          op=ALU.add),
                          deps=[a3, a4] + bw_dt)
                sp1.read(a5); sp2.read(a5)
                dt_t.wrote(a5)
                state["dt_tok"][dc] = a5
                a6 = p.op("vector", lambda e, dc=dc:
                          e.tensor_tensor(out=dtx.h[:, dc, :],
                                          in0=dt_t.h[:, dc, :],
                                          in1=xs_c.h[:, dc, :],
                                          op=ALU.mult),
                          deps=[a5] + xs_c.wr + dtx.rd + dtx.wr)
                xs_c.read(a6)
                dt_t.read(a6)
                dtx.rd = []
                dtx.wrote(a6)
                state["dtx_tok"][dc] = a6

        def scan_chunk(gc, gen=None):
            sz_c = sz_p[gc % 2]
            ybuf_toks = []
            for dc in range(PC_DI):
                ysum_tok = None
                for g in range(2):
                    s0 = g * 8
                    dA3 = dA3p[state["unit"] % 2]
                    state["unit"] += 1
                    bw_dA = dA3.begin_write()
                    exp_toks = []
                    for j in range(8):
                        s = s0 + j
                        tk = p.op("scalar", lambda e, dc=dc, s=s, j=j,
                                  dA3=dA3: e.activation(
                                      out=dA3.h[:, j, :],
                                      in_=dt_t.h[:, dc, :], func=AF.Exp,
                                      scale=A_w.h[:, dc, s:s + 1]),
                                  deps=bw_dA + [state["dt_tok"][dc]]
                                  + A_w.wr)
                        dt_t.read(tk)
                        A_w.read(tk)
                        dA3.wrote(tk)
                        exp_toks.append(tk)
                    bw_dB = dB3.begin_write()
                    tB_toks = []
                    for j in range(8):
                        s = s0 + j
                        tk = p.op("vector", lambda e, dc=dc, s=s, j=j:
                                  e.tensor_tensor(
                                      out=dB3.h[:, j, :],
                                      in0=dtx.h[:, dc, :],
                                      in1=Bball.h[:, s, :], op=ALU.mult),
                                  deps=bw_dB + [state["dtx_tok"][dc]]
                                  + Bball.wr)
                        dtx.read(tk)
                        Bball.read(tk)
                        tB_toks.append(tk)
                    dB3.wr = list(tB_toks); dB3.rd = []
                    blk = dc * 2 + g
                    f1 = p.op("vector", lambda e, blk=blk, dA3=dA3:
                              e.tensor_tensor(
                                  out=f_t.h[:], in0=dA3.h[:, :, 0:1],
                                  in1=hcar.h[:, blk, :, :], op=ALU.mult),
                              deps=exp_toks + hcar.wr + f_t.begin_write())
                    f_t.wrote(f1)
                    hcar.read(f1)
                    f2 = p.op("vector", lambda e: e.tensor_tensor(
                        out=dB3.h[:, :, 0:1], in0=dB3.h[:, :, 0:1],
                        in1=f_t.h[:], op=ALU.add),
                        deps=[f1] + tB_toks)
                    f_t.read(f2)
                    dB3.wr = dB3.wr + [f2]
                    f3 = p.op("vector", lambda e, dA3=dA3: e.memset(
                        dA3.h[:, :, 0:1], 0.0), deps=[f1])
                    dA3.wrote(f3)
                    th = p.op("vector", lambda e, dA3=dA3:
                              e.tensor_tensor_scan(
                                  out=dA3.h[:].rearrange("p a b -> p (a b)"),
                                  data0=dA3.h[:].rearrange("p a b -> p (a b)"),
                                  data1=dB3.h[:].rearrange("p a b -> p (a b)"),
                                  initial=zc.h[:], op0=ALU.mult,
                                  op1=ALU.add),
                              deps=exp_toks + [f2, f3] + zc.wr
                              + dB3.wr)
                    dA3.read(th)
                    dA3.wrote(th)
                    dB3.read(th)
                    zc.read(th)
                    scan_toks = [th]
                    tcr = p.op("vector", lambda e, blk=blk, dA3=dA3:
                               e.tensor_copy(
                                   out=hcar.h[:, blk, :, :],
                                   in_=dA3.h[:, :, TCS - 1:TCS]),
                               deps=scan_toks + hcar.rd + hcar.wr)
                    dA3.read(tcr)
                    hcar.wr = [tcr]
                    hcar.rd = []
                    tpr = p.op("vector", lambda e, s0=s0, dA3=dA3:
                               e.tensor_tensor(
                                   out=dB3.h[:], in0=dA3.h[:],
                                   in1=Cball.h[:, s0:s0 + 8, :],
                                   op=ALU.mult),
                               deps=scan_toks + dB3.rd + dB3.wr
                               + Cball.wr + [tcr])
                    Cball.read(tpr)
                    dA3.read(tpr)
                    dB3.wr = [tpr]; dB3.rd = []
                    for (w0, w1) in ((4, 4), (2, 2), (1, 1)):
                        tk = p.op("vector", lambda e, w0=w0:
                                  e.tensor_tensor(
                                      out=dB3.h[:, 0:w0, :],
                                      in0=dB3.h[:, 0:w0, :],
                                      in1=dB3.h[:, w0:2 * w0, :],
                                      op=ALU.add),
                                  deps=dB3.wr)
                        dB3.wr = dB3.wr + [tk]
                    if g == 0:
                        ysum_tok = p.op("vector", lambda e: e.tensor_copy(
                            out=ysum.h[:], in_=dB3.h[:, 0, :]),
                            deps=dB3.wr + ysum.begin_write())
                    else:
                        ysum_tok = p.op("vector", lambda e: e.tensor_tensor(
                            out=ysum.h[:], in0=ysum.h[:],
                            in1=dB3.h[:, 0, :], op=ALU.add),
                            deps=dB3.wr + [ysum_tok])
                    dB3.read(ysum_tok)
                    ysum.wrote(ysum_tok)
                t1 = p.op("vector", lambda e, dc=dc:
                          e.scalar_tensor_tensor(
                              out=ytmp.h[:], in0=xs_c.h[:, dc, :],
                              scalar=dp_w.h[:, dc:dc + 1], in1=ysum.h[:],
                              op0=ALU.mult, op1=ALU.add),
                          deps=[ysum_tok] + xs_c.wr + ytmp.begin_write()
                          + dp_w.wr)
                xs_c.read(t1)
                ysum.read(t1)
                dp_w.read(t1)
                ytmp.wrote(t1)
                t2 = p.op("vector", lambda e, dc=dc, sz_c=sz_c:
                          e.tensor_tensor(
                              out=dtx.h[:, dc, :], in0=ytmp.h[:],
                              in1=sz_c.h[:, dc, :], op=ALU.mult),
                          deps=[t1] + sz_c.wr + dtx.rd + dtx.wr)
                ytmp.read(t2)
                sz_c.read(t2)
                dtx.wrote(t2)
                ybuf_toks.append(t2)
                if gen is not None:
                    next(gen, None)
                    next(gen, None)
            state["ybuf_toks"] = ybuf_toks

        def out_chunk(gc):
            ybuf_toks = state["ybuf_toks"]
            ev = []
            for oc8 in range(PC_D):
                dma_load(wol, outw[widx][:, oc8], cast=True)
                pso = pp.acquire()
                dmo = pso.begin_write() + wol.wr
                tok = None
                for kc in range(PC_DI):
                    tok = p.op("tensor", lambda e, kc=kc, pso=pso: e.matmul(
                        pso.h[:], wol.h[:, kc, :], dtx.h[:, kc, :],
                        start=(kc == 0), stop=(kc == PC_DI - 1)),
                        deps=dmo + [ybuf_toks[kc]])
                    wol.read(tok)
                    dtx.read(tok)
                pso.wrote(tok)
                par = oc8 % 2
                if not rev:
                    fo_s = st_f[par]
                    tk = p.op("scalar", lambda e, pso=pso, fo_s=fo_s:
                              e.activation(out=fo_s.h[:], in_=pso.h[:],
                                           func=AF.Copy),
                              deps=pso.wr + fo_s.begin_write())
                    pso.read(tk)
                    fo_s.wrote(tk)
                    st = dma_store(fwdsp[:, gc, oc8, :], fo_s)
                    dram_tok["fwdsp"][gc].append(st)
                else:
                    foc_s, xc4_s = st_f[par], st_x[0]
                    mtmp_s, gt_s = st_m[0], st_g[0]
                    dma_load(foc_s, fwdsp[:, gc, oc8, :],
                             deps=[dram_tok["fwdsp"][gc]])
                    dma_load(xc4_s, xasp[:, gc, oc8, :], cast=True)
                    m1_ = p.op("vector", lambda e, pso=pso, foc_s=foc_s,
                               mtmp_s=mtmp_s: e.tensor_tensor(
                                   out=mtmp_s.h[:], in0=pso.h[:],
                                   in1=foc_s.h[:], op=ALU.add),
                               deps=pso.wr + foc_s.wr
                               + mtmp_s.begin_write())
                    pso.read(m1_)
                    foc_s.read(m1_)
                    mtmp_s.wrote(m1_)
                    m2_ = p.op("vector", lambda e, mtmp_s=mtmp_s,
                               xc4_s=xc4_s: e.tensor_tensor(
                                   out=mtmp_s.h[:], in0=mtmp_s.h[:],
                                   in1=xc4_s.h[:], op=ALU.add),
                               deps=[m1_] + xc4_s.wr)
                    xc4_s.read(m2_)
                    mtmp_s.wr = [m2_]
                    m3_ = p.op("scalar", lambda e, gt_s=gt_s, mtmp_s=mtmp_s:
                               e.activation(out=gt_s.h[:], in_=mtmp_s.h[:],
                                            func=AF.Gelu),
                               deps=[m2_] + gt_s.begin_write())
                    mtmp_s.read(m3_)
                    gt_s.wrote(m3_)
                    m4_ = p.op("vector", lambda e, xc4_s=xc4_s, gt_s=gt_s,
                               mtmp_s=mtmp_s: e.tensor_tensor(
                                   out=mtmp_s.h[:], in0=xc4_s.h[:],
                                   in1=gt_s.h[:], op=ALU.add),
                               deps=[m3_] + mtmp_s.wr + xc4_s.wr)
                    gt_s.read(m4_)
                    xc4_s.read(m4_)
                    mtmp_s.wrote(m4_)
                    st = dma_store(xspB[:, gc, oc8, :], mtmp_s,
                                   eng="gpsimd")
                    dram_tok["xspB"][gc].append(st)
                ev.append(st)
            return ev

        g0 = prep_gen(0)
        for _ in g0:
            pass
        prep_b(0)
        for gc in range(4):
            gen = prep_gen(gc + 1) if gc + 1 < 4 else None
            scan_chunk(gc, gen)
            if gen is not None:
                for _ in gen:
                    pass
            out_chunk(gc)
            if gc + 1 < 4:
                prep_b(gc + 1)

        for b in (xs_c, sz_p[0], sz_p[1], xsraw, hcar, zc, f_t, dtx,
                  dtr16, Brow, Crow, ysum, ytmp, rtmp, ctmp, dt_t, Bball,
                  Cball, dA3p[0], dA3p[1], dB3, wil[0], wil[1], wol,
                  *st_f, *st_x, *st_m, *st_g):
            b.free()


    def phase_ffn():
        xc5 = mk("p6_xc", [128, 8, 1024], F32)
        xn5 = mk("p6_xn", [128, 8, 1024], BF16)
        for oc in range(2):
            bw_x5 = xc5.begin_write()
            for half in range(2):
                ci = 2 * oc + half
                dma_load_into(xc5, xc5.h[:, :, half * 512:half * 512 + 512],
                              xspB[:, ci],
                              deps=[dram_tok["xspB"][ci]] + bw_x5)
            bw_xn = xn5.begin_write()
            for sc in range(2):
                s0 = sc * 512

                def xv(i, s0=s0):
                    return xc5.h[:, i, s0:s0 + 512]

                psM, psR = ln_512(xc5, xv)
                lnt2b = None

                def ow(i, lv, deps, s0=s0, psR=psR):
                    tok2 = p.op("vector", lambda e, i=i, s0=s0, psR=psR,
                                lv=lv: e.tensor_tensor(out=lv, in0=lv,
                                                       in1=psR.h[:],
                                                       op=ALU.mult),
                                deps=deps)
                    lns["lnt"].wrote(tok2)
                    tok3 = p.op("vector", lambda e, i=i, s0=s0, lv=lv:
                                e.tensor_scalar(
                                    out=xn5.h[:, i, s0:s0 + 512],
                                    in0=lv,
                                    scalar1=fg_sb.h[:, i:i + 1],
                                    scalar2=fb_sb.h[:, i:i + 1],
                                    op0=ALU.mult, op1=ALU.add),
                                deps=[tok2] + bw_xn + fg_sb.wr + fb_sb.wr)
                    lns["lnt"].read(tok3)
                    xn5.wrote(tok3)
                    return tok3

                ln_apply(xc5, xv, psM, psR, ow)
            x1t = mk("p6_x1t", [128, 8, 1024], F32)
            dma_load(x1t, xspA[:, oc], deps=[dram_tok["xspA"][2 * oc],
                                             dram_tok["xspA"][2 * oc + 1]])
            tadd = p.op("vector", lambda e, x1t=x1t: e.tensor_tensor(
                out=xc5.h[:], in0=xc5.h[:], in1=x1t.h[:], op=ALU.add),
                deps=xc5.wr + xc5.rd + x1t.wr)
            x1t.read(tadd)
            xc5.wr = xc5.wr + [tadd]
            x1t.free()
            hfb = mk(f"p6_hfb{oc}", [128, 32, 1024], BF16)
            f1l = [mk(f"p6_f1l{oc}_{i}", [128, 8, 512], BF16)
                   for i in range(2)]
            bw_h = hfb.begin_write()
            for sl in range(8):
                fl = f1l[sl % 2]
                dma_load(fl, ff1[:, sl], cast=True)
                for mb in range(4):
                    m = sl * 4 + mb
                    for tc in range(2):
                        c0 = tc * 512
                        ps = pp.acquire()
                        dmm = ps.begin_write() + fl.wr + xn5.wr
                        tok = None
                        for kc in range(PC_D):
                            tok = p.op("tensor", lambda e, kc=kc, mb=mb,
                                       c0=c0, fl=fl, ps=ps: e.matmul(
                                           ps.h[:],
                                           fl.h[:, kc, mb * 128:(mb + 1) * 128],
                                           xn5.h[:, kc, c0:c0 + 512],
                                           start=(kc == 0),
                                           stop=(kc == PC_D - 1)), deps=dmm)
                            fl.read(tok)
                            xn5.read(tok)
                        ps.wrote(tok)
                        tk = p.op("scalar", lambda e, m=m, c0=c0, ps=ps,
                                  hfb=hfb: e.activation(
                                      out=hfb.h[:, m, c0:c0 + 512],
                                      in_=ps.h[:], func=AF.Gelu,
                                      bias=f1b_sb.h[:, m:m + 1]),
                                  deps=ps.wr + bw_h + f1b_sb.wr)
                        ps.read(tk)
                        hfb.wrote(tk)
            for fl in f1l:
                fl.free()
            ev = []
            for dg in range(2):
                wf2h = [mk(f"p6_wf2h{oc}_{dg}_{i}", [128, 16, 512], BF16)
                        for i in range(2)]
                for i, wpc in enumerate(wf2h):
                    dw2 = wpc.begin_write()
                    for piece in range(2):
                        k0 = piece * 8
                        tok = p.dma("gpsimd", lambda e, k0=k0, dg=dg, i=i,
                                    wpc=wpc: e.dma_start(
                                        out=wpc.h[:, k0:k0 + 8, :],
                                        in_=ff2[:, dg, i * 16 + k0:
                                                i * 16 + k0 + 8, :]),
                                    deps=dw2)
                        wpc.wrote(tok)
                for tc in range(2):
                    c0 = tc * 512
                    pses = [pp.acquire() for _ in range(4)]
                    dps = [ps.begin_write() + hfb.wr + wf2h[0].wr
                           + wf2h[1].wr for ps in pses]
                    lasts = [None] * 4
                    for k in range(PC_HF):
                        wpc = wf2h[k // 16]
                        kk = k % 16
                        for j in range(4):
                            ps = pses[j]
                            tok = p.op("tensor", lambda e, kk=kk, j=j, c0=c0,
                                       k=k, ps=ps, wpc=wpc, hfb=hfb:
                                       e.matmul(
                                           ps.h[:],
                                           wpc.h[:, kk, j * 128:(j + 1) * 128],
                                           hfb.h[:, k, c0:c0 + 512],
                                           start=(k == 0),
                                           stop=(k == PC_HF - 1)),
                                       deps=dps[j])
                            hfb.read(tok)
                            wpc.read(tok)
                            lasts[j] = tok
                    for j in range(4):
                        do = dg * 4 + j
                        ps = pses[j]
                        ps.wrote(lasts[j])
                        tok = p.op("vector", lambda e, do=do, c0=c0, ps=ps:
                                   e.scalar_tensor_tensor(
                                       out=xc5.h[:, do, c0:c0 + 512],
                                       in0=ps.h[:],
                                       scalar=f2b_sb.h[:, do:do + 1],
                                       in1=xc5.h[:, do, c0:c0 + 512],
                                       op0=ALU.add, op1=ALU.add),
                                   deps=ps.wr + xc5.wr + f2b_sb.wr)
                        ps.read(tok)
                        xc5.wrote(tok)
                        ev.append(tok)
                wf2h[0].free()
                wf2h[1].free()
            hfb.free()
            st = dma_store(xspA[:, oc], xc5, deps=ev)
            dram_tok["xspA"][2 * oc] = [st]
            dram_tok["xspA"][2 * oc + 1] = [st]
        for b in (xc5, xn5):
            b.free()

    final_sts = []
    if "p2" in phases:
        final_sts += macaron_half("p2", xin, xspA,
                                  [dram_tok["xspA"][i] for i in range(4)])
    if "p3" in phases:
        phase_h2()
        final_sts = [t for c in dram_tok["h2sp"] + dram_tok["xasp"]
                     for t in c]
    lns_free()
    if "p4" in phases:
        mamba_dir("p4", 1, False)
        final_sts = [t for c in dram_tok["fwdsp"] for t in c]
    if "p5" in phases:
        mamba_dir("p5", 2, True)
        final_sts = [t for c in dram_tok["xspB"] for t in c]
    lns_alloc()
    if "p6" in phases:
        phase_ffn()
        final_sts = [t for c in dram_tok["xspA"] for t in c]
    if "p7" in phases:
        final_sts = macaron_half("p7", xspA, out, None, last=True)

    p.fence("sync", [final_sts])
    p.finalize_and_emit(nc)
    return nc


def _host_inputs(inputs):
    f = np.ascontiguousarray

    def vec(v, pc):
        return f(np.asarray(v).reshape(pc, 128).T)

    def vmat(v, pc, w):
        return f(np.asarray(v).reshape(pc, 128, w).transpose(1, 0, 2))

    g = {}
    for k, pre in ((1, "L0_attn_m1_"), (2, "L0_attn_m2_")):
        in_w = np.asarray(inputs[pre + "in_w"])
        g[f"inw{k}"] = f(in_w.reshape(16, 256, 8, 128).transpose(3, 0, 2, 1))
        out_w = np.asarray(inputs[pre + "out_w"])
        g[f"outw{k}"] = f(out_w.reshape(8, 128, 16, 128)
                          .transpose(3, 0, 2, 1))
        xp = np.asarray(inputs[pre + "xproj_w"])
        g[f"xpd{k}"] = f(xp[0:DTR].T.reshape(16, 128, DTR).transpose(1, 0, 2))
        g[f"xpb{k}"] = f(xp[DTR:DTR + DST].T.reshape(16, 128, DST)
                         .transpose(1, 0, 2))
        g[f"xpc{k}"] = f(xp[DTR + DST:].T.reshape(16, 128, DST)
                         .transpose(1, 0, 2))
        dt_w = np.asarray(inputs[pre + "dt_w"])
        g[f"dtw{k}"] = f(dt_w.reshape(16, 128, DTR).transpose(2, 0, 1))
        g[f"conv{k}"] = vmat(inputs[pre + "conv_w"], 16, 4)
        g[f"cb{k}"] = vec(inputs[pre + "conv_b"], 16)
        g[f"dtb{k}"] = vec(inputs[pre + "dt_b"], 16)
        g[f"alog{k}"] = vmat(inputs[pre + "Alog"], 16, DST)
        g[f"dp{k}"] = vec(inputs[pre + "D"], 16)
    g["ng"] = vec(inputs["L0_attn_norm_g"], 8)
    g["nb"] = vec(inputs["L0_attn_norm_b"], 8)
    g["fg"] = vec(inputs["L0_attn_ffn_g"], 8)
    g["fb"] = vec(inputs["L0_attn_ffn_b"], 8)
    ff1w = np.asarray(inputs["L0_attn_ff1_w"])
    g["ff1"] = f(ff1w.reshape(8, 512, 8, 128).transpose(3, 0, 2, 1))
    g["f1b"] = vec(inputs["L0_attn_ff1_b"], 32)
    ff2w = np.asarray(inputs["L0_attn_ff2_w"])
    g["ff2"] = f(ff2w.reshape(2, 512, 32, 128)
                 .transpose(3, 0, 2, 1))
    g["f2b"] = vec(inputs["L0_attn_ff2_b"], 8)
    wgm = np.asarray(inputs["L0_ffn_w_gate"])
    g["wg"] = f(wgm.reshape(12, 256, 8, 128).transpose(3, 0, 2, 1))
    wum = np.asarray(inputs["L0_ffn_w_up"])
    g["wu"] = f(wum.reshape(12, 256, 8, 128).transpose(3, 0, 2, 1))
    wdm = np.asarray(inputs["L0_ffn_w_down"])
    g["wd"] = f(wdm.T.reshape(24, 128, 1024).transpose(1, 0, 2))
    g["eyeF"] = np.eye(128, dtype=np.float32)
    g["jmatF"] = f(np.eye(128, dtype=np.float32)[::-1])
    sel = np.zeros((DST, DST, 128), dtype=np.float32)
    for s in range(DST):
        sel[s, s, :] = 1.0
    g["selF"] = sel
    return g


def _stage_x(xb):
    return np.ascontiguousarray(
        np.asarray(xb).reshape(2, 1024, 8, 128).transpose(3, 0, 2, 1))


def _unstage_out(o):
    return np.asarray(o).transpose(1, 3, 2, 0).reshape(2048, 1024)


_NC_CACHE = {}


def _get_nc():
    if "nc" not in _NC_CACHE:
        _NC_CACHE["nc"] = build()
    return _NC_CACHE["nc"]


def kernel(**inputs):
    nc = _get_nc()
    g = _host_inputs(inputs)
    x = np.asarray(inputs["L0_attn_x"])
    in_maps = []
    for b in range(NCORES):
        m = dict(g)
        m["xin"] = _stage_x(x[b])
        in_maps.append(m)
    res = run_bass_kernel_spmd(nc, in_maps, core_ids=list(range(NCORES)))
    outs = [_unstage_out(r["out"]) for r in res.results]
    return np.stack(outs, axis=0).astype(np.float32)
